# version 04 (final)
# baseline (speedup 1.0000x reference)
import contextlib
import math
import numpy as np
import ml_dtypes

import concourse.bass as bass
import concourse.mybir as mybir
from concourse.bass_utils import run_bass_kernel_spmd

F32 = mybir.dt.float32
BF16 = mybir.dt.bfloat16
AF = mybir.ActivationFunctionType
ALU = mybir.AluOpType
AX = mybir.AxisListType

NCORES = 8
B_FULL = 16
S = 2048
D = 1024
BPC = B_FULL // NCORES
T = BPC * S
NT = D // 128
TCH = 512
NCH = T // TCH

ENGINES = ("sync", "tensor", "vector", "scalar", "gpsimd")
DMA_ENGINES = {"sync", "gpsimd", "scalar"}


def nm(x):
    if isinstance(x, str):
        return x
    return x.name


def rtile(ap, j):
    return f"{nm(ap)}#t{j}"


class _Op:
    __slots__ = ("idx", "eng", "fn", "deps", "dma", "inc_amt", "inc_cum", "needs_inc", "tag", "dma_slot", "dma_seq")

    def __init__(self, idx, eng, fn, deps, dma, tag):
        self.idx = idx
        self.eng = eng
        self.fn = fn
        self.deps = deps
        self.dma = dma
        self.inc_amt = 16 if dma else 1
        self.inc_cum = None
        self.needs_inc = False
        self.tag = tag


class Prog:

    def __init__(self, nc):
        self.nc = nc
        self.ops = []
        self.eng_ops = {e: [] for e in ENGINES}
        self.last_write = {}
        self.readers = {}
        self.es = contextlib.ExitStack()
        self.sbuf_bytes = 0

    def sbuf(self, name, shape, dtype):
        if not hasattr(self, "_names"):
            self._names = {}
            self._cms = {}
        if name in self._names:
            self._names[name] += 1
            name = f"{name}_v{self._names[name]}"
        else:
            self._names[name] = 0
        cm = self.nc.sbuf_tensor(name, list(shape), dtype)
        t = cm.__enter__()
        self._cms[nm(t)] = cm
        per_part = int(np.prod(shape[1:])) * (2 if dtype == BF16 else 4)
        self.sbuf_bytes += per_part
        return t

    def push_scope(self):
        if not hasattr(self, "_scopes"):
            self._scopes = []
        self._cms = getattr(self, "_cms", {})
        self._scopes.append(list(self._cms.keys()))

    def pop_scope(self):
        base = set(self._scopes.pop())
        live = [k for k in self._cms.keys() if k not in base]
        for k in reversed(live):
            self._cms.pop(k).__exit__(None, None, None)

    def free(self, *tensors):
        for t in tensors:
            cm = self._cms.pop(nm(t))
            cm.__exit__(None, None, None)

    def close_sbuf(self):
        for cm in reversed(list(getattr(self, "_cms", {}).values())):
            cm.__exit__(None, None, None)
        self._cms = {}

    def barrier(self, tag="barrier"):
        deps = []
        for e in ENGINES:
            if self.eng_ops[e]:
                deps.append(self.eng_ops[e][-1])
        for o2 in getattr(self, "_dmas_since_barrier", []):
            deps.append(o2)
        o = _Op(len(self.ops), "vector", lambda e: e.engine_nop(), deps, False, tag)
        self.ops.append(o)
        self.eng_ops["vector"].append(o)
        self._barrier_op = o
        self._dmas_since_barrier = []
        return o

    def psum(self, name, shape, dtype=F32):
        return self.es.enter_context(self.nc.psum_tensor(name, list(shape), dtype))

    def op(self, eng, fn, reads=(), writes=(), dma=False, tag=""):
        deps = []
        seen = set()

        def add(o):
            if o is not None and o.idx not in seen:
                seen.add(o.idx)
                deps.append(o)

        rnames = [nm(r) for r in reads]
        wnames = [nm(w) for w in writes]
        for r in rnames:
            add(self.last_write.get(r))
        for w in wnames:
            add(self.last_write.get(w))
            for o in self.readers.get(w, ()):
                add(o)
        bo = getattr(self, "_barrier_op", None)
        if bo is not None and bo.idx not in seen:
            deps.append(bo)
        o = _Op(len(self.ops), eng, fn, deps, dma, tag)
        self.ops.append(o)
        self.eng_ops[eng].append(o)
        if dma:
            if not hasattr(self, "_dmas_since_barrier"):
                self._dmas_since_barrier = []
            self._dmas_since_barrier.append(o)
        for r in rnames:
            self.readers.setdefault(r, []).append(o)
        for w in wnames:
            self.last_write[w] = o
            self.readers[w] = []
        return o

    N_DMA_SEMS = 48

    def lower(self):
        nc = self.nc
        for o in self.ops:
            if o.dma:
                o.needs_inc = True
            for d in o.deps:
                d.needs_inc = True
        per_q = 16
        dma_seq = {}
        qcount = {}
        n_dma = 0
        for o in self.ops:
            if o.dma:
                qi = list(ENGINES).index(o.eng)
                k = qcount.get(o.eng, 0)
                qcount[o.eng] = k + 1
                slot = qi * per_q + (k % per_q)
                o.dma_slot = slot
                o.dma_seq = dma_seq.get(slot, 0)
                dma_seq[slot] = o.dma_seq + 1
                n_dma += 1
        totals = {}
        for e in ENGINES:
            c = 0
            for o in self.eng_ops[e]:
                if o.needs_inc and not o.dma:
                    c += 1
                o.inc_cum = c
            totals[e] = c

        EP = 24000
        sems = {}
        for e in ENGINES:
            n_ep = max(1, (totals[e] + EP - 1) // EP)
            sems[e] = [self.es.enter_context(nc.semaphore(f"sem_{e}{i}"))
                       for i in range(n_ep)]
        used_slots = sorted({o.dma_slot for o in self.ops if o.dma})
        dsems = {}
        for s in used_slots:
            dsems[s] = self.es.enter_context(nc.semaphore(f"dsem{s}"))

        def dep_key_val(d):
            if d.dma:
                return ("d", d.dma_slot), 16 * (d.dma_seq + 1)
            ep = (d.inc_cum - 1) // EP
            return ("e", d.eng, ep), d.inc_cum - ep * EP

        def dep_sem(d):
            if d.dma:
                return dsems[d.dma_slot]
            return sems[d.eng][(d.inc_cum - 1) // EP]

        block = self.es.enter_context(nc.Block())

        def emit_for(eng_name):
            def run(eng):
                waited = {}
                for o in self.eng_ops[eng_name]:
                    for d in o.deps:
                        k, v = dep_key_val(d)
                        assert d.needs_inc and v is not None and v > 0, (o.tag, d.tag)
                        if waited.get(k, 0) < v:
                            eng.wait_ge(dep_sem(d), v)
                            waited[k] = v
                    if o.dma:
                        k = ("d", o.dma_slot)
                        gate = 16 * o.dma_seq
                        if gate > 0 and waited.get(k, 0) < gate:
                            eng.wait_ge(dsems[o.dma_slot], gate)
                            waited[k] = gate
                    inst = o.fn(eng)
                    if o.dma:
                        inst.then_inc(dsems[o.dma_slot], 16)
                    elif o.needs_inc:
                        assert inst is not None, f"op {o.tag} needs inc but emitted no inst"
                        inst.then_inc(sems[o.eng][(o.inc_cum - 1) // EP], 1)
            return run

        for e in ENGINES:
            if not self.eng_ops[e]:
                continue
            run = emit_for(e)
            if e == "sync":
                block.sync(run)
            elif e == "tensor":
                block.tensor(run)
            elif e == "vector":
                block.vector(run)
            elif e == "scalar":
                block.scalar(run)
            elif e == "gpsimd":
                block.gpsimd(run)
        return sems, totals

    def dma(self, q, out, in_, reads=(), writes=(), tag="dma", store=False, **kw):
        o = self.op(q, lambda e: e.dma_start(out=out, in_=in_, **kw),
                    reads=list(reads), writes=list(writes), dma=True, tag=tag)
        if store:
            if not hasattr(self, "store_ops"):
                self.store_ops = []
            self.store_ops.append(o)
        return o

    def finish(self):
        stores = getattr(self, "store_ops", [])
        o = _Op(len(self.ops), "sync", lambda e: e.nop(), list(stores), False, "join")
        self.ops.append(o)
        self.eng_ops["sync"].append(o)

    def act(self, out, in_, func, bias=0.0, scale=1.0, accum_out=None,
            extra_reads=(), extra_writes=(), eng="scalar", tag="act"):
        def f(e):
            return e.activation(out=out, in_=in_, func=func, bias=bias,
                                scale=scale, accum_out=accum_out)
        reads = [in_] + list(extra_reads)
        if isinstance(bias, bass.AP):
            reads.append(bias)
        if isinstance(scale, bass.AP):
            reads.append(scale)
        writes = [out] + list(extra_writes)
        if accum_out is not None:
            writes.append(accum_out)
        return self.op(eng, f, reads=reads, writes=writes, tag=tag)

    def tt(self, out, a, b, alu, eng="vector", tag="tt"):
        return self.op(eng, lambda e: e.tensor_tensor(out, a, b, alu),
                       reads=[a, b], writes=[out], tag=tag)

    def ts(self, out, in0, s1, s2, op0, op1=ALU.bypass, eng="vector", tag="ts"):
        reads = [in0]
        if isinstance(s1, bass.AP):
            reads.append(s1)
        if isinstance(s2, bass.AP):
            reads.append(s2)
        return self.op(eng, lambda e: e.tensor_scalar(out, in0, s1, s2, op0, op1),
                       reads=reads, writes=[out], tag=tag)

    def stt(self, out, in0, scalar, in1, op0, op1, eng="vector", tag="stt"):
        reads = [in0, in1]
        if isinstance(scalar, bass.AP):
            reads.append(scalar)
        return self.op(eng, lambda e: e.scalar_tensor_tensor(out, in0, scalar, in1, op0, op1),
                       reads=reads, writes=[out], tag=tag)

    def copy(self, out, in_, eng="vector", tag="copy"):
        return self.op(eng, lambda e: e.tensor_copy(out, in_),
                       reads=[in_], writes=[out], tag=tag)

    def memset(self, out, v, eng="vector", tag="memset"):
        return self.op(eng, lambda e: e.memset(out, v), reads=[], writes=[out], tag=tag)

    def reduce(self, out, in_, alu, axis=AX.X, eng="vector", tag="red", **kw):
        return self.op(eng, lambda e: e.tensor_reduce(out, in_, axis, alu, **kw),
                       reads=[in_], writes=[out], tag=tag)

    def recip(self, out, in_, tag="recip"):
        return self.op("vector", lambda e: e.reciprocal(out, in_),
                       reads=[in_], writes=[out], tag=tag)

    def scan(self, out, d0, d1, init, op0, op1, tag="scan"):
        return self.op("vector", lambda e: e.tensor_tensor_scan(out, d0, d1, init, op0, op1),
                       reads=[d0, d1], writes=[out], tag=tag)

    def mm(self, bank_ap, pairs, start=True, stop=True, extra_reads=(), tag="mm"):
        def f(e):
            inst = None
            n = len(pairs)
            for i, (lh, rh) in enumerate(pairs):
                st = start and i == 0
                sp = stop and i == n - 1
                inst = e.matmul(bank_ap, lhsT=lh, rhs=rh, start=st, stop=sp)
            return inst
        reads = [p[0] for p in pairs] + [p[1] for p in pairs] + list(extra_reads)
        return self.op("tensor", f, reads=reads, writes=[bank_ap], tag=tag)

    def transpose(self, out_psum, in_sbuf, ident, tag="tr"):
        return self.op("tensor", lambda e: e.transpose(out_psum, in_sbuf, ident),
                       reads=[in_sbuf, ident], writes=[out_psum], tag=tag)



class Ctx:

    def __init__(self, p: Prog, consts: dict, need_xbf=True, need_rw=True):
        self.p = p
        nc = p.nc
        self.consts = consts
        self.xbf = [p.sbuf(f"xbf{i}", (128, T), BF16) for i in range(NT)] if need_xbf else None
        self.abc = p.sbuf("abc", (128, T), F32)
        self.bbc = p.sbuf("bbc", (128, T), F32)
        self.rw = [p.sbuf(f"rw{i}", (128, T), F32) for i in range(2)] if need_rw else None
        self.rA = p.sbuf("rowA", (1, T), F32)
        self.rB = p.sbuf("rowB", (1, T), F32)
        self.rscr = p.sbuf("rscr", (1, 3 * TCH), F32)
        self.scr = [p.sbuf(f"scr{i}", (128, TCH), F32) for i in range(4)]
        self.onec = p.sbuf("onec", (128, 1), F32)
        self.onecb = p.sbuf("onecb", (128, 1), BF16)
        self.oner = p.sbuf("oner", (1, TCH), F32)
        self.onec1 = p.sbuf("onec1", (1, 128), F32)
        self.negon = p.sbuf("negon", (1, 128), BF16)
        self.eye = p.sbuf("eye", (128, 128), BF16)
        self.eyeb = self.eye
        self.hmask = p.sbuf("hmask", (128, 128), F32)
        self.hmask8 = p.sbuf("hmask8", (128, 64), F32)
        self.banks = [p.psum(f"pb{i}", (128, 512)) for i in range(7)]
        self.ptr = p.psum("ptr", (128, 1024), BF16)
        self._bank_i = 0
        self._uid = 0

    def uid(self):
        self._uid += 1
        return self._uid

    def bank(self, n=1, hold=False):
        if not hasattr(self, "_held"):
            self._held = set()
        out = []
        tries = 0
        while len(out) < n:
            b = self.banks[self._bank_i]
            self._bank_i = (self._bank_i + 1) % 7
            tries += 1
            assert tries < 32, "psum banks exhausted (too many holds)"
            if nm(b) in self._held:
                continue
            if hold:
                self._held.add(nm(b))
            out.append(b)
        return out if n > 1 else out[0]

    def release(self, *banks):
        for b in banks:
            self._held.discard(nm(b))

    def load_consts(self):
        p = self.p
        p.dma("sync", self.onec[:], self.consts["ones_col"], writes=[self.onec], tag="c0")
        p.dma("sync", self.oner[:], self.consts["ones_row"], writes=[self.oner], tag="c1")
        p.dma("sync", self.onec1[:], self.consts["ones_r128"], writes=[self.onec1], tag="c2")
        p.dma("sync", self.eye[:], self.consts["eye_bf16"], writes=[self.eye], tag="c3")
        p.dma("sync", self.onecb[:], self.consts["ones_col_bf"], writes=[self.onecb], tag="c4")
        p.dma("sync", self.negon[:], self.consts["negones_bf"], writes=[self.negon], tag="c5")
        p.dma("sync", self.hmask[:], self.consts["hmask16"], writes=[self.hmask], tag="c6")
        p.dma("sync", self.hmask8[:], self.consts["hmask8"], writes=[self.hmask8], tag="c7")

    def rows_to_bcast(self):
        p = self.p
        for c in range(NCH):
            sl = slice(c * TCH, (c + 1) * TCH)
            for row, dst in ((self.rA, self.abc), (self.rB, self.bbc)):
                bk = self.bank()
                p.mm(bk[:, 0:TCH], [(self.onec1[:], row[0:1, sl])],
                     tag=f"rbc_{nm(dst)}_{c}")
                p.copy(dst[:, sl], bk[:, 0:TCH], tag=f"rbce_{nm(dst)}_{c}")

    def head(self, r_in, with_f32=False):
        p = self.p
        self.rows_to_bcast()
        for i in range(NT):
            buf = self.rw[i % 2]
            p.dma("sync", buf[:], r_in[i * 128:(i + 1) * 128, :], reads=[rtile(r_in, i)], writes=[buf], tag=f"hld{i}")
            t1 = self.scr[i % 2]
            for c in range(NCH):
                sl = slice(c * TCH, (c + 1) * TCH)
                p.tt(t1[:, 0:TCH], buf[:, sl], self.abc[:, sl], ALU.mult, tag=f"hm{i}_{c}")
                p.tt(self.xbf[i][:, sl], t1[:, 0:TCH], self.bbc[:, sl], ALU.add, tag=f"ha{i}_{c}")

    def stats_reset(self):
        self.p.memset(self.rA[:], 0.0, tag="st0a")
        self.p.memset(self.rB[:], 0.0, tag="st0b")

    def tail_chunk(self, tile_i, ch, rprime_ap):
        p = self.p
        sl = slice(ch * TCH, (ch + 1) * TCH)
        sq = self.scr[2]
        p.act(sq[:, 0:TCH], rprime_ap, AF.Square, tag=f"tsq{tile_i}_{ch}")
        b1 = self.bank()
        p.mm(b1[0:1, 0:TCH], [(self.onec[:], rprime_ap)], tag=f"tsum{tile_i}_{ch}")
        p.stt(self.rA[0:1, sl], b1[0:1, 0:TCH], 1.0, self.rA[0:1, sl],
              ALU.mult, ALU.add, tag=f"tsa{tile_i}_{ch}")
        b2 = self.bank()
        p.mm(b2[0:1, 0:TCH], [(self.onec[:], sq[:, 0:TCH])], tag=f"tsq2{tile_i}_{ch}")
        p.stt(self.rB[0:1, sl], b2[0:1, 0:TCH], 1.0, self.rB[0:1, sl],
              ALU.mult, ALU.add, tag=f"tsb{tile_i}_{ch}")

    def tail_finish(self):
        p = self.p
        for c in range(NCH):
            sl = slice(c * TCH, (c + 1) * TCH)
            s0 = self.rscr[0:1, 0:TCH]
            s1 = self.rscr[0:1, TCH:2 * TCH]
            s2 = self.rscr[0:1, 2 * TCH:3 * TCH]
            p.ts(s0, self.rA[0:1, sl], 1.0 / D, None, ALU.mult, tag=f"tf_mean{c}")
            p.ts(s1, self.rB[0:1, sl], 1.0 / D, None, ALU.mult, tag=f"tf_ex2{c}")
            p.act(s2, s0, AF.Square, tag=f"tf_m2{c}")
            p.tt(s1, s1, s2, ALU.subtract, tag=f"tf_var{c}")
            p.ts(s1, s1, 1e-5, None, ALU.add, tag=f"tf_eps{c}")
            p.act(s1, s1, AF.Sqrt, tag=f"tf_sd{c}")
            p.recip(self.rA[0:1, sl], s1, tag=f"tf_a{c}")
            p.tt(self.rB[0:1, sl], s0, self.rA[0:1, sl], ALU.mult, tag=f"tf_mb{c}")
            p.ts(self.rB[0:1, sl], self.rB[0:1, sl], -1.0, None, ALU.mult, tag=f"tf_b{c}")


def _l3_selj():
    m = np.zeros((128, 2048), ml_dtypes.bfloat16)
    for j in range(16):
        for pp2 in range(128):
            m[pp2, j * 128 + 8 * j + pp2 // 16] = 1.0
    return m


def make_consts():
    hm16 = np.zeros((128, 128), np.float32)
    for j in range(8):
        for pp in range(128):
            hm16[pp, j * 16 + 2 * j + pp // 64] = 1.0
    hm8 = np.zeros((128, 64), np.float32)
    for j in range(8):
        for pp in range(128):
            hm8[pp, j * 8 + j] = 1.0
    return {
        "ones_col": np.ones((128, 1), np.float32),
        "ones_row": np.ones((1, TCH), np.float32),
        "ones_r128": np.ones((1, 128), np.float32),
        "eye_bf16": np.eye(128, dtype=ml_dtypes.bfloat16),
        "ones_col_bf": np.ones((128, 1), ml_dtypes.bfloat16),
        "ones_row_bf": np.ones((1, TCH), ml_dtypes.bfloat16),
        "ones_r128_bf": np.ones((1, 128), ml_dtypes.bfloat16),
        "eye_f32": np.eye(128, dtype=np.float32),
        "l3_selJ": _l3_selj(),
        "l3_repB": np.ascontiguousarray(
            (np.arange(128)[None, :] % 16 == np.arange(16)[:, None]).astype(ml_dtypes.bfloat16)),
        "negones_bf": -np.ones((1, 128), ml_dtypes.bfloat16),
        "hmask16": hm16,
        "hmaskT16": np.ascontiguousarray(
            np.concatenate([hm16[:, j * 16:(j + 1) * 16].T for j in range(8)], axis=1)),
        "hmask8": hm8,
    }


def declare_consts(nc, consts):
    out = {}
    for k, v in consts.items():
        dt = BF16 if v.dtype == ml_dtypes.bfloat16 else F32
        out[k] = nc.dram_tensor(f"c_{k}", list(v.shape), dt, kind="ExternalInput").ap()
    return out



def dma_kc_split(p, q, dst_view3, src_view3, k, tag, max_k=8, **kw):
    for k0 in range(0, k, max_k):
        k1 = min(k0 + max_k, k)
        p.dma(q, dst_view3[:, k0:k1, :], src_view3[:, k0:k1, :], tag=f"{tag}_{k0}", **kw)


def load_w_ft(C, hbm_ap, rows, cols, name, dtype=BF16, q="gpsimd"):
    p = C.p
    nk = rows // 128
    tiles = []
    for k in range(nk):
        t = p.sbuf(f"w_{name}_{k}", (128, cols), dtype)
        p.dma(q, t[:], hbm_ap[k * 128:(k + 1) * 128, :], writes=[t], tag=f"wld_{name}_{k}")
        tiles.append(t)
    return tiles


def load_vec_cols(C, hbm_ap, n, name, q="sync"):
    p = C.p
    t = p.sbuf(f"v_{name}", (128, n // 128), F32)
    src = hbm_ap.rearrange("(a p) -> p a", p=128)
    p.dma(q, t[:], src, writes=[t], tag=f"vld_{name}",
          allow_slow_non_contiguous=True)
    return t


def load_vec_row(C, hbm_ap, n, name, q="sync"):
    p = C.p
    t = p.sbuf(f"vr_{name}", (1, n), F32)
    p.dma(q, t[:], hbm_ap.unsqueeze(0), writes=[t], tag=f"vrld_{name}")
    return t



def emit_L1_ffn(C, W, r_in, r_out):
    p = C.p
    wr = load_w_ft(C, W["L1_ffn_W_r_T"], D, D, "l1f_wr")
    wk = load_w_ft(C, W["L1_ffn_W_k_T"], D, D, "l1f_wk")
    wv = load_w_ft(C, W["L1_ffn_W_v_T"], D, D, "l1f_wv")
    mu_r = load_vec_cols(C, W["L1_ffn_mu_r"], D, "l1f_mur")
    mu_k = load_vec_cols(C, W["L1_ffn_mu_k"], D, "l1f_muk")

    C.rows_to_bcast()
    C.stats_reset()

    x32 = [p.sbuf(f"l1f_x32_{i}", (128, TCH), F32) for i in range(NT)]
    seam = p.sbuf("l1f_seam", (128, NT), F32)
    seam2 = p.sbuf("l1f_seam2", (128, NT), F32)
    xprev0 = p.sbuf("l1f_xp0", (128, NT), F32)
    xr_bf = [p.sbuf(f"l1f_xr_{i}", (128, TCH), BF16) for i in range(NT)]
    xk_bf = [p.sbuf(f"l1f_xk_{i}", (128, TCH), BF16) for i in range(NT)]
    rg32 = [p.sbuf(f"l1f_rg_{i}", (128, TCH), F32) for i in range(NT)]
    ka_bf = [p.sbuf(f"l1f_ka_{i}", (128, TCH), BF16) for i in range(NT)]
    rbuf = [p.sbuf(f"l1f_rb_{i}", (128, TCH), F32) for i in range(2)]
    robuf = [p.sbuf(f"l1f_ro_{i}", (128, TCH), F32) for i in range(2)]
    dscr = p.sbuf("l1f_dscr", (128, TCH), F32)

    p.memset(seam[:], 0.0, tag="l1f_seam0")

    for c in range(NCH):
        sl = slice(c * TCH, (c + 1) * TCH)
        row_start = (c * TCH) % S == 0
        for i in range(NT):
            rb = rbuf[i % 2]
            p.dma("sync", rb[:], r_in[i * 128:(i + 1) * 128, sl], reads=[rtile(r_in, i)], writes=[rb], tag=f"l1f_ld{c}_{i}")
            p.tt(dscr[:], rb[:], C.abc[:, sl], ALU.mult, tag=f"l1f_xa{c}_{i}")
            p.copy(seam2[:, i:i + 1], x32[i][:, TCH - 1:TCH], tag=f"l1f_sv{c}_{i}") if c > 0 else None
            p.tt(x32[i][:], dscr[:], C.bbc[:, sl], ALU.add, tag=f"l1f_xb{c}_{i}")
            if row_start:
                p.memset(xprev0[:, i:i + 1], 0.0, tag=f"l1f_z{c}_{i}")
            else:
                p.copy(xprev0[:, i:i + 1], seam2[:, i:i + 1], tag=f"l1f_sc{c}_{i}")
            p.tt(dscr[:, 1:TCH], x32[i][:, 1:TCH], x32[i][:, 0:TCH - 1], ALU.subtract, tag=f"l1f_d{c}_{i}")
            p.tt(dscr[:, 0:1], x32[i][:, 0:1], xprev0[:, i:i + 1], ALU.subtract, tag=f"l1f_d0{c}_{i}")
            p.stt(xr_bf[i][:, 1:TCH], dscr[:, 1:TCH], mu_r[:, i:i + 1], x32[i][:, 0:TCH - 1],
                  ALU.mult, ALU.add, tag=f"l1f_xr{c}_{i}")
            p.stt(xr_bf[i][:, 0:1], dscr[:, 0:1], mu_r[:, i:i + 1], xprev0[:, i:i + 1],
                  ALU.mult, ALU.add, tag=f"l1f_xr0{c}_{i}")
            p.stt(xk_bf[i][:, 1:TCH], dscr[:, 1:TCH], mu_k[:, i:i + 1], x32[i][:, 0:TCH - 1],
                  ALU.mult, ALU.add, tag=f"l1f_xk{c}_{i}")
            p.stt(xk_bf[i][:, 0:1], dscr[:, 0:1], mu_k[:, i:i + 1], xprev0[:, i:i + 1],
                  ALU.mult, ALU.add, tag=f"l1f_xk0{c}_{i}")
        for j in range(NT):
            bk = C.bank()
            p.mm(bk[:, 0:TCH], [(wr[kc][:, j * 128:(j + 1) * 128], xr_bf[kc][:]) for kc in range(NT)],
                 tag=f"l1f_mr{c}_{j}")
            p.act(rg32[j][:], bk[:, 0:TCH], AF.Sigmoid, tag=f"l1f_sig{c}_{j}")
            bk2 = C.bank()
            p.mm(bk2[:, 0:TCH], [(wk[kc][:, j * 128:(j + 1) * 128], xk_bf[kc][:]) for kc in range(NT)],
                 tag=f"l1f_mk{c}_{j}")
            p.ts(dscr[:], bk2[:, 0:TCH], 0.0, None, ALU.max, tag=f"l1f_rel{c}_{j}")
            p.act(ka_bf[j][:], dscr[:], AF.Square, tag=f"l1f_sq{c}_{j}")
        for j in range(NT):
            bk = C.bank()
            p.mm(bk[:, 0:TCH], [(wv[kc][:, j * 128:(j + 1) * 128], ka_bf[kc][:]) for kc in range(NT)],
                 tag=f"l1f_mv{c}_{j}")
            ro = robuf[j % 2]
            p.tt(dscr[:], rg32[j][:], bk[:, 0:TCH], ALU.mult, tag=f"l1f_o{c}_{j}")
            p.tt(ro[:], x32[j][:], dscr[:], ALU.add, tag=f"l1f_r{c}_{j}")
            C.tail_chunk(j, c, ro[:])
            p.dma("sync", r_out[j * 128:(j + 1) * 128, sl], ro[:], reads=[ro], writes=[rtile(r_out, j)], store=True, tag=f"l1f_st{c}_{j}")
    C.tail_finish()



def prep_weights(inputs):
    w = {}
    for k in ("L1_ffn_W_r", "L1_ffn_W_k", "L1_ffn_W_v"):
        if k in inputs:
            w[k + "_T"] = np.ascontiguousarray(inputs[k].T)
    for k in ("L1_ffn_mu_r", "L1_ffn_mu_k"):
        if k in inputs:
            w[k] = np.ascontiguousarray(inputs[k])
    if "L0_attn_W_qkv" in inputs:
        wqkv = inputs["L0_attn_W_qkv"]
        idx = np.arange(3072).reshape(16, 3, 64)
        w["L0_attn_Wq"] = np.ascontiguousarray(wqkv[:, idx[:, 0].reshape(-1)])
        w["L0_attn_Wk"] = np.ascontiguousarray(wqkv[:, idx[:, 1].reshape(-1)])
        w["L0_attn_Wv"] = np.ascontiguousarray(wqkv[:, idx[:, 2].reshape(-1)])
        w["L0_attn_W_out"] = np.ascontiguousarray(inputs["L0_attn_W_out"])
    if "L1_attn_Wq" in inputs:
        for k in ("L1_attn_Wq", "L1_attn_Wk", "L1_attn_Wv", "L1_attn_Wo",
                  "L1_attn_W_down_kv", "L1_attn_W_up_k", "L1_attn_W_up_v"):
            w[k] = np.ascontiguousarray(inputs[k])
        pp, yy = np.meshgrid(np.arange(128), np.arange(4608), indexing="ij")
        w["L1_decay_neg"] = (-1e-4 * np.abs(pp + 2048 - yy)).astype(ml_dtypes.bfloat16)
    if "L0_ffn_Wr" in inputs:
        w["L0_ffn_Wr_T"] = np.ascontiguousarray(inputs["L0_ffn_Wr"].T)
        for k in ("L0_ffn_br", "L0_ffn_biases_N", "L0_ffn_Wl1", "L0_ffn_bl1",
                  "L0_ffn_Wl2", "L0_ffn_bl2"):
            w[k] = np.ascontiguousarray(inputs[k])
    if "L2_attn_Wq" in inputs:
        for k in ("ln_g", "ln_b", "Wq", "bq", "Wk", "bk", "Wv", "bv", "Wo", "bo"):
            w[f"L2_attn_{k}"] = np.ascontiguousarray(inputs[f"L2_attn_{k}"])
    if "L3_attn_fwd_in_w" in inputs:
        for pref in ("fwd", "bwd"):
            w[f"L3_{pref}_in_w_T"] = np.ascontiguousarray(inputs[f"L3_attn_{pref}_in_w"].T)
            w[f"L3_{pref}_x_w_T"] = np.ascontiguousarray(inputs[f"L3_attn_{pref}_x_w"].T)
            w[f"L3_{pref}_dt_w_T"] = np.ascontiguousarray(inputs[f"L3_attn_{pref}_dt_w"].T)
            w[f"L3_{pref}_out_w_T"] = np.ascontiguousarray(inputs[f"L3_attn_{pref}_out_w"].T)
            for k in ("conv_w", "conv_b", "dt_b", "A_log", "D"):
                w[f"L3_{pref}_{k}"] = np.ascontiguousarray(inputs[f"L3_attn_{pref}_{k}"])
        for k in ("ln_g", "ln_b", "w1", "b1", "w2", "b2"):
            kk = f"L3_attn_ff_{k}"
            if k in ("w1", "w2"):
                w[f"L3_ff_{k}_T"] = np.ascontiguousarray(inputs[kk].T)
            else:
                w[f"L3_ff_{k}"] = np.ascontiguousarray(inputs[kk])
    if "L3_ffn_rail_maker" in inputs:
        rm = inputs["L3_ffn_rail_maker"]
        up = inputs["L3_ffn_rail_up_proj"]
        bdm = np.zeros((4, 8, 128, 128), np.float32)
        bdu = np.zeros((4, 8, 4, 128, 128), np.float32)
        for h in range(4):
            for g in range(8):
                for r in range(8):
                    bdm[h, g, r * 16:(r + 1) * 16, r * 16:(r + 1) * 16] = rm[h, g * 8 + r]
                for cc in range(4):
                    for rr in range(2):
                        r = 2 * cc + rr
                        bdu[h, g, cc, r * 16:(r + 1) * 16, rr * 64:(rr + 1) * 64] = \
                            up[h, g * 8 + r].T
        w["L3_ffn_bd_maker"] = bdm
        w["L3_ffn_bd_up"] = bdu
        w["L3_ffn_w_cproj_T"] = np.ascontiguousarray(inputs["L3_ffn_w_cproj"].T)
    if "L2_ffn_q_w" in inputs:
        for k in ("q_w", "k_w", "v_w", "o_w"):
            w[f"L2_ffn_{k}_T"] = np.ascontiguousarray(inputs[f"L2_ffn_{k}"].T)
        for k in ("q_b", "k_b", "v_b", "o_b", "norm_w", "norm_b"):
            w[f"L2_ffn_{k}"] = np.ascontiguousarray(inputs[f"L2_ffn_{k}"])
    return w


BLOCK_WEIGHTS = {
    "L1_ffn": ["L1_ffn_W_r_T", "L1_ffn_W_k_T", "L1_ffn_W_v_T", "L1_ffn_mu_r", "L1_ffn_mu_k"],
    "L0_attn": ["L0_attn_Wq", "L0_attn_Wk", "L0_attn_Wv", "L0_attn_W_out"],
    "L1_attn": ["L1_attn_Wq", "L1_attn_Wk", "L1_attn_Wv", "L1_attn_Wo",
                "L1_attn_W_down_kv", "L1_attn_W_up_k", "L1_attn_W_up_v", "L1_decay_neg"],
    "L0_ffn": ["L0_ffn_Wr_T", "L0_ffn_br", "L0_ffn_biases_N", "L0_ffn_Wl1",
               "L0_ffn_bl1", "L0_ffn_Wl2", "L0_ffn_bl2"],
    "L3_ffn": ["L3_ffn_bd_maker", "L3_ffn_bd_up", "L3_ffn_w_cproj_T"],
    "L3_attn": (["L3_" + pref + "_" + k for pref in ("fwd", "bwd")
                 for k in ("in_w_T", "x_w_T", "dt_w_T", "out_w_T", "conv_w", "conv_b",
                           "dt_b", "A_log", "D")]
                + ["L3_ff_ln_g", "L3_ff_ln_b", "L3_ff_w1_T", "L3_ff_b1",
                   "L3_ff_w2_T", "L3_ff_b2"]),
    "L2_attn": ["L2_attn_ln_g", "L2_attn_ln_b", "L2_attn_Wq", "L2_attn_bq",
                "L2_attn_Wk", "L2_attn_bk", "L2_attn_Wv", "L2_attn_bv",
                "L2_attn_Wo", "L2_attn_bo"],
    "L2_ffn": ["L2_ffn_q_w_T", "L2_ffn_k_w_T", "L2_ffn_v_w_T", "L2_ffn_o_w_T",
               "L2_ffn_q_b", "L2_ffn_k_b", "L2_ffn_v_b", "L2_ffn_o_b",
               "L2_ffn_norm_w", "L2_ffn_norm_b"],
}

BLOCK_NEED_XBF = set()

BLOCK_EMITTERS = {}


def _register():
    BLOCK_EMITTERS["L1_ffn"] = emit_L1_ffn
    BLOCK_EMITTERS["L0_attn"] = emit_L0_attn
    BLOCK_EMITTERS["L1_attn"] = emit_L1_attn
    BLOCK_EMITTERS["L2_ffn"] = emit_L2_ffn
    BLOCK_EMITTERS["L0_ffn"] = emit_L0_ffn
    BLOCK_EMITTERS["L3_ffn"] = emit_L3_ffn
    BLOCK_EMITTERS["L2_attn"] = emit_L2_attn
    BLOCK_EMITTERS["L3_attn"] = emit_L3_attn


def build_single_block(block_name, wshapes, consts_np):
    nc = bass.Bass("TRN2", target_bir_lowering=False, debug=False)
    r_in = nc.dram_tensor("r_in", [D, T], F32, kind="ExternalInput").ap()
    rows_in = nc.dram_tensor("rows_in", [2, T], F32, kind="ExternalInput").ap()
    r_out = nc.dram_tensor("r_out", [D, T], F32, kind="ExternalOutput").ap()
    rows_out = nc.dram_tensor("rows_out", [2, T], F32, kind="ExternalOutput").ap()
    consts_aps = declare_consts(nc, consts_np)
    W = {}
    for k in BLOCK_WEIGHTS[block_name]:
        sh, dt = wshapes[k]
        W[k] = nc.dram_tensor(f"w_{k}", list(sh), dt, kind="ExternalInput").ap()
    p = Prog(nc)
    C = Ctx(p, consts_aps, need_xbf=block_name in BLOCK_NEED_XBF,
            need_rw=block_name in BLOCK_NEED_XBF)
    C.load_consts()
    p.dma("sync", C.rA[:], rows_in[0:1, :], writes=[C.rA], tag="rows_ld_a")
    p.dma("sync", C.rB[:], rows_in[1:2, :], writes=[C.rB], tag="rows_ld_b")
    BLOCK_EMITTERS[block_name](C, W, r_in, r_out)
    p.dma("sync", rows_out[0:1, :], C.rA[:], reads=[C.rA], store=True, tag="rows_st_a")
    p.dma("sync", rows_out[1:2, :], C.rB[:], reads=[C.rB], store=True, tag="rows_st_b")
    p.finish()
    p.lower()
    p.close_sbuf()
    p.es.close()
    return nc


def host_shard_x(x):
    out = []
    for c in range(NCORES):
        xc = x[c * BPC:(c + 1) * BPC].reshape(T, D)
        out.append(np.ascontiguousarray(xc.T))
    return out


def host_unshard_x(rs):
    full = np.empty((B_FULL, S, D), np.float32)
    for c in range(NCORES):
        full[c * BPC:(c + 1) * BPC] = rs[c].T.reshape(BPC, S, D)
    return full





def proj_chunk_xbf(C, r_in, gch, xbf_c, rb_pool):
    p = C.p
    sl = slice(gch * TCH, (gch + 1) * TCH)
    for kc in range(NT):
        rb = rb_pool[kc % 2]
        p.dma("sync", rb[:], r_in[kc * 128:(kc + 1) * 128, sl],
              reads=[rtile(r_in, kc)], writes=[rb], tag=f"pc_ld{gch}_{kc}")
        s = C.scr[kc % 2]
        p.tt(s[:, 0:TCH], rb[:], C.abc[:, sl], ALU.mult, tag=f"pc_m{gch}_{kc}")
        p.tt(xbf_c[kc][:], s[:, 0:TCH], C.bbc[:, sl], ALU.add, tag=f"pc_a{gch}_{kc}")


def attn_core(C, b, n_heads, dh, q_scr, k_scr, v_scr, mub_sb, kmax2,
              attn_out, srow, negm_row, qt, kt_, vt, v_tm, pt_pool,
              rsum_sb, decay_neg=None, tagp=""):
    p = C.p
    hpt = 128 // dh
    n_kt = S // 128
    for hg in range(n_heads // hpt):
        for t, scr_ in ((qt, q_scr), (kt_, k_scr), (vt, v_scr)):
            p.dma("sync", t[:], scr_[hg * 128:(hg + 1) * 128, b * S:(b + 1) * S],
                  reads=[f"{nm(scr_)}#t{hg}"], writes=[t], tag=f"at{tagp}_ld{b}_{hg}")
        for kt in range(n_kt):
            tp = C.ptr
            slot = kt % 8
            p.transpose(tp[:, slot * 128:(slot + 1) * 128],
                        vt[:, kt * 128:(kt + 1) * 128], C.eye[:],
                        tag=f"at{tagp}_tr{b}_{hg}_{kt}")
            p.copy(v_tm[:, kt * 128:(kt + 1) * 128], tp[:, slot * 128:(slot + 1) * 128],
                   tag=f"at{tagp}_tre{b}_{hg}_{kt}")
        for h2 in range(hpt):
            h = hg * hpt + h2
            base = h2 * dh
            assert base in (0, 64) or hpt == 1
            p.dma("gpsimd", negm_row[:], mub_sb[h:h + 1, b * S:(b + 1) * S],
                  reads=[mub_sb], writes=[negm_row], tag=f"at{tagp}_nm{b}_{h}")
            for qch in range(S // TCH):
                qsl = slice(qch * TCH, (qch + 1) * TCH)
                av = C.bank(hold=True)
                sb_ = C.bank(hold=True)
                for kt in range(n_kt):
                    st = C.bank()
                    def emit_st(e, st=st, kt=kt, qsl=qsl, base=base, qch=qch):
                        i1 = e.matmul(st[:, 0:TCH],
                                      lhsT=kt_[base:base + dh, kt * 128:(kt + 1) * 128],
                                      rhs=qt[base:base + dh, qsl], start=True, stop=False)
                        if decay_neg is not None:
                            off = 2048 - kt * 128 + qch * TCH
                            e.matmul(st[:, 0:TCH], lhsT=C.eyeb[:],
                                     rhs=decay_neg[:, off:off + TCH],
                                     start=False, stop=False)
                        i2 = e.matmul(st[:, 0:TCH], lhsT=C.negon[:],
                                      rhs=negm_row[0:1, qsl], start=False, stop=True)
                        return i2
                    reads = [kt_, qt, negm_row, C.negon]
                    if decay_neg is not None:
                        reads += [C.eyeb, decay_neg]
                    p.op("tensor", emit_st, reads=reads, writes=[st], tag=f"at{tagp}_st{b}_{h}_{qch}_{kt}")
                    ptile = pt_pool[kt % 2]
                    p.act(ptile[:], st[:, 0:TCH], AF.Exp, tag=f"at{tagp}_exp{b}_{h}_{qch}_{kt}")
                    p.mm(av[0:dh, 0:TCH],
                         [(v_tm[:, kt * 128 + base:kt * 128 + base + dh], ptile[:])],
                         start=(kt == 0), stop=(kt == n_kt - 1), tag=f"at{tagp}_av{b}_{h}_{qch}_{kt}")
                    p.mm(sb_[0:1, 0:TCH], [(C.onecb[:], ptile[:])],
                         start=(kt == 0), stop=(kt == n_kt - 1), tag=f"at{tagp}_sm{b}_{h}_{qch}_{kt}")
                C.release(av, sb_)
                p.copy(srow[0:1, 0:TCH], sb_[0:1, 0:TCH], tag=f"at{tagp}_sr{b}_{h}_{qch}")
                p.recip(srow[0:1, 0:TCH], srow[0:1, 0:TCH], tag=f"at{tagp}_rc{b}_{h}_{qch}")
                bc = C.bank()
                p.mm(bc[0:dh, 0:TCH], [(C.onec1[0:1, 0:dh], srow[0:1, 0:TCH])],
                     tag=f"at{tagp}_bc{b}_{h}_{qch}")
                p.copy(rsum_sb[0:dh, 0:TCH], bc[0:dh, 0:TCH], tag=f"at{tagp}_bce{b}_{h}_{qch}")
                p.stt(attn_out[hg][base:base + dh, qch * TCH:(qch + 1) * TCH],
                      av[0:dh, 0:TCH], 1.0, rsum_sb[0:dh, 0:TCH], ALU.mult, ALU.mult,
                      tag=f"at{tagp}_ev{b}_{h}_{qch}")


def emit_L0_attn(C, W, r_in, r_out):
    p = C.p
    nc = p.nc
    q_scr = nc.dram_tensor("l0a_qscr", [D, T], BF16, kind="Internal").ap()
    k_scr = nc.dram_tensor("l0a_kscr", [D, T], BF16, kind="Internal").ap()
    v_scr = nc.dram_tensor("l0a_vscr", [D, T], BF16, kind="Internal").ap()

    C.rows_to_bcast()
    C.stats_reset()

    H, DH = 16, 64
    qn2 = p.sbuf("l0a_qn2", (H, T), F32)
    kmax2 = p.sbuf("l0a_kmax2", (H, BPC), F32)
    rb_pool = [p.sbuf(f"l0a_rb{i}", (128, TCH), F32) for i in range(2)]
    robuf = [p.sbuf(f"l0a_ro{i}", (128, TCH), F32) for i in range(2)]
    wq = load_w_ft(C, W["L0_attn_Wq"], D, D, "l0a_wq")
    wk = load_w_ft(C, W["L0_attn_Wk"], D, D, "l0a_wk")
    wv = load_w_ft(C, W["L0_attn_Wv"], D, D, "l0a_wv")
    xbf_c = [p.sbuf(f"l0a_xc{i}", (128, TCH), BF16) for i in range(NT)]
    stage = [p.sbuf(f"l0a_stg{i}", (128, TCH), BF16) for i in range(3)]
    sqs = p.sbuf("l0a_sqs", (128, TCH), F32)

    p.memset(kmax2[:], 0.0, tag="l0a_km0")

    for b in range(BPC):
        for ch in range(S // TCH):
            gch = b * (S // TCH) + ch
            sl = slice(gch * TCH, (gch + 1) * TCH)
            proj_chunk_xbf(C, r_in, gch, xbf_c, rb_pool)
            qn2b = C.bank(hold=True)
            kn2b = C.bank(hold=True)
            for j in range(NT):
                hm = C.hmask[:, j * H:(j + 1) * H]
                for wt, scr_, st_i, scale in ((wq, q_scr, 0, 0.125), (wk, k_scr, 1, 1.0), (wv, v_scr, 2, 1.0)):
                    bk = C.bank()
                    p.mm(bk[:, 0:TCH],
                         [(wt[kc][:, j * 128:(j + 1) * 128], xbf_c[kc][:]) for kc in range(NT)],
                         tag=f"l0a_p{gch}_{j}_{st_i}")
                    stg = stage[st_i]
                    p.act(stg[:], bk[:, 0:TCH], AF.Copy, scale=scale, tag=f"l0a_pe{gch}_{j}_{st_i}")
                    p.dma("sync", scr_[j * 128:(j + 1) * 128, sl], stg[:],
                          reads=[stg], writes=[f"{nm(scr_)}#t{j}"], tag=f"l0a_ps{gch}_{j}_{st_i}")
                    if st_i == 0:
                        p.act(sqs[:], stg[:], AF.Square, tag=f"l0a_sq{gch}_{j}")
                        p.mm(qn2b[0:H, 0:TCH], [(hm, sqs[:])], start=(j == 0), stop=(j == NT - 1),
                             tag=f"l0a_qn{gch}_{j}")
                    elif st_i == 1:
                        p.act(sqs[:], stg[:], AF.Square, tag=f"l0a_sqk{gch}_{j}")
                        p.mm(kn2b[0:H, 0:TCH], [(hm, sqs[:])], start=(j == 0), stop=(j == NT - 1),
                             tag=f"l0a_kn{gch}_{j}")
            p.copy(qn2[:, sl], qn2b[0:H, 0:TCH], tag=f"l0a_qne{gch}")
            p.reduce(C.scr[3][0:H, 0:1], kn2b[0:H, 0:TCH], ALU.max, tag=f"l0a_knm{gch}")
            p.tt(kmax2[:, b:b + 1], kmax2[:, b:b + 1], C.scr[3][0:H, 0:1], ALU.max,
                 tag=f"l0a_knu{gch}")
            C.release(qn2b, kn2b)
    for b in range(BPC):
        bsl = slice(b * S, (b + 1) * S)
        p.ts(qn2[:, bsl], qn2[:, bsl], kmax2[:, b:b + 1], None, ALU.mult, tag=f"l0a_mu{b}")
        p.act(qn2[:, bsl], qn2[:, bsl], AF.Sqrt, tag=f"l0a_mus{b}")

    p.barrier(tag="l0a_ph1_end")
    p.free(sqs, *stage[::-1], *xbf_c[::-1], *wv[::-1], *wk[::-1], *wq[::-1])

    wo = load_w_ft(C, W["L0_attn_W_out"], D, D, "l0a_wo")
    qt = p.sbuf("l0a_qt", (128, S), BF16)
    ktt = p.sbuf("l0a_kt", (128, S), BF16)
    vt = p.sbuf("l0a_vt", (128, S), BF16)
    v_tm = p.sbuf("l0a_vtm", (128, S), BF16)
    pt_pool = [p.sbuf(f"l0a_pt{i}", (128, TCH), BF16) for i in range(2)]
    srow = p.sbuf("l0a_srow", (1, TCH), F32)
    negm_row = p.sbuf("l0a_negm", (1, S), BF16)
    rsum_sb = p.sbuf("l0a_rsum", (128, TCH), F32)
    attn_out = [p.sbuf(f"l0a_ao{i}", (128, S), BF16) for i in range(NT)]

    for b in range(BPC):
        attn_core(C, b, H, DH, q_scr, k_scr, v_scr, qn2, kmax2,
                  attn_out, srow, negm_row, qt, ktt, vt, v_tm, pt_pool,
                  rsum_sb, tagp="l0")
        for ch in range(S // TCH):
            gch = b * (S // TCH) + ch
            sl = slice(gch * TCH, (gch + 1) * TCH)
            csl = slice(ch * TCH, (ch + 1) * TCH)
            for j in range(NT):
                bk = C.bank()
                p.mm(bk[:, 0:TCH],
                     [(wo[kc][:, j * 128:(j + 1) * 128], attn_out[kc][:, csl]) for kc in range(NT)],
                     tag=f"l0a_o{gch}_{j}")
                rb = rb_pool[j % 2]
                p.dma("sync", rb[:], r_in[j * 128:(j + 1) * 128, sl],
                      reads=[rtile(r_in, j)], writes=[rb], tag=f"l0a_tl{gch}_{j}")
                s = C.scr[j % 2]
                p.tt(s[:, 0:TCH], rb[:], C.abc[:, sl], ALU.mult, tag=f"l0a_t1{gch}_{j}")
                p.tt(s[:, 0:TCH], s[:, 0:TCH], C.bbc[:, sl], ALU.add, tag=f"l0a_t2{gch}_{j}")
                ro = robuf[j % 2]
                p.tt(ro[:], s[:, 0:TCH], bk[:, 0:TCH], ALU.add, tag=f"l0a_t3{gch}_{j}")
                C.tail_chunk(j, gch, ro[:])
                p.dma("sync", r_out[j * 128:(j + 1) * 128, sl], ro[:], reads=[ro],
                      writes=[rtile(r_out, j)], store=True, tag=f"l0a_st{gch}_{j}")
    C.tail_finish()



def emit_L1_attn(C, W, r_in, r_out):
    p = C.p
    nc = p.nc
    H, DH = 16, 64
    SC = float(1.0 / (np.sqrt(np.float32(1024)) + 1e-8))
    q_scr = nc.dram_tensor("l1a_qscr", [D, T], BF16, kind="Internal").ap()
    kv_scr = nc.dram_tensor("l1a_kvscr", [D, T], BF16, kind="Internal").ap()
    k_scr = nc.dram_tensor("l1a_kscr", [D, T], BF16, kind="Internal").ap()
    v_scr = nc.dram_tensor("l1a_vscr", [D, T], BF16, kind="Internal").ap()

    C.rows_to_bcast()
    C.stats_reset()

    qn2 = p.sbuf("l1a_qn2", (H, T), F32)
    kmax2 = p.sbuf("l1a_kmax2", (H, BPC), F32)
    rb_pool = [p.sbuf(f"l1a_rb{i}", (128, TCH), F32) for i in range(2)]
    robuf = [p.sbuf(f"l1a_ro{i}", (128, TCH), F32) for i in range(2)]
    latent = [p.sbuf(f"l1a_lat{i}", (128, T), BF16) for i in range(2)]
    sqs = p.sbuf("l1a_sqs", (128, TCH), F32)
    p.memset(kmax2[:], 0.0, tag="l1a_km0")

    wq = load_w_ft(C, W["L1_attn_Wq"], D, D, "l1a_wq")
    wk = load_w_ft(C, W["L1_attn_Wk"], D, D, "l1a_wk")
    wv = load_w_ft(C, W["L1_attn_Wv"], D, D, "l1a_wv")
    xbf_c = [p.sbuf(f"l1a_xc{i}", (128, TCH), BF16) for i in range(NT)]
    stage = [p.sbuf(f"l1a_stg{i}", (128, TCH), BF16) for i in range(3)]
    for gch in range(NCH):
        sl = slice(gch * TCH, (gch + 1) * TCH)
        proj_chunk_xbf(C, r_in, gch, xbf_c, rb_pool)
        qn2b = C.bank(hold=True)
        for j in range(NT):
            hm = C.hmask[:, j * H:(j + 1) * H]
            bkq = C.bank()
            p.mm(bkq[:, 0:TCH], [(wq[kc][:, j * 128:(j + 1) * 128], xbf_c[kc][:]) for kc in range(NT)],
                 tag=f"l1a_pq{gch}_{j}")
            p.act(stage[0][:], bkq[:, 0:TCH], AF.Copy, scale=SC, tag=f"l1a_peq{gch}_{j}")
            p.dma("sync", q_scr[j * 128:(j + 1) * 128, sl], stage[0][:],
                  reads=[stage[0]], writes=[f"{nm(q_scr)}#t{j}"], tag=f"l1a_psq{gch}_{j}")
            p.act(sqs[:], stage[0][:], AF.Square, tag=f"l1a_sq{gch}_{j}")
            p.mm(qn2b[0:H, 0:TCH], [(hm, sqs[:])], start=(j == 0), stop=(j == NT - 1),
                 tag=f"l1a_qn{gch}_{j}")
            bkk = C.bank()
            p.mm(bkk[:, 0:TCH], [(wk[kc][:, j * 128:(j + 1) * 128], xbf_c[kc][:]) for kc in range(NT)],
                 tag=f"l1a_pk{gch}_{j}")
            p.act(stage[1][:], bkk[:, 0:TCH], AF.Copy, tag=f"l1a_pek{gch}_{j}")
            bkv = C.bank()
            p.mm(bkv[:, 0:TCH], [(wv[kc][:, j * 128:(j + 1) * 128], xbf_c[kc][:]) for kc in range(NT)],
                 tag=f"l1a_pv{gch}_{j}")
            p.act(stage[2][:], bkv[:, 0:TCH], AF.Copy, tag=f"l1a_pev{gch}_{j}")
            p.tt(stage[1][:], stage[1][:], stage[2][:], ALU.add, tag=f"l1a_kv{gch}_{j}")
            p.dma("sync", kv_scr[j * 128:(j + 1) * 128, sl], stage[1][:],
                  reads=[stage[1]], writes=[f"{nm(kv_scr)}#t{j}"], tag=f"l1a_pskv{gch}_{j}")
        p.copy(qn2[:, sl], qn2b[0:H, 0:TCH], tag=f"l1a_qne{gch}")
        C.release(qn2b)
    p.barrier(tag="l1a_ph1_end")
    p.free(*stage[::-1], *xbf_c[::-1], *wv[::-1], *wk[::-1], *wq[::-1])

    wd = load_w_ft(C, W["L1_attn_W_down_kv"], D, 256, "l1a_wd")
    kvt = [p.sbuf(f"l1a_kvt{i}", (128, TCH), BF16) for i in range(NT)]
    lst = p.sbuf("l1a_lst", (128, TCH), BF16)
    for gch in range(NCH):
        sl = slice(gch * TCH, (gch + 1) * TCH)
        for kc in range(NT):
            p.dma("sync", kvt[kc][:], kv_scr[kc * 128:(kc + 1) * 128, sl],
                  reads=[f"{nm(kv_scr)}#t{kc}"], writes=[kvt[kc]], tag=f"l1a_lkv{gch}_{kc}")
        for lj in range(2):
            bk = C.bank()
            p.mm(bk[:, 0:TCH], [(wd[kc][:, lj * 128:(lj + 1) * 128], kvt[kc][:]) for kc in range(NT)],
                 tag=f"l1a_lat{gch}_{lj}")
            p.act(latent[lj][:, sl], bk[:, 0:TCH], AF.Copy, tag=f"l1a_late{gch}_{lj}")
    p.barrier(tag="l1a_ph1b_end")
    p.free(lst, *kvt[::-1], *wd[::-1])

    wuk = load_w_ft(C, W["L1_attn_W_up_k"], 256, D, "l1a_wuk")
    wuv = load_w_ft(C, W["L1_attn_W_up_v"], 256, D, "l1a_wuv")
    stage2 = [p.sbuf(f"l1a_st2{i}", (128, TCH), BF16) for i in range(2)]
    for gch in range(NCH):
        b = gch // (S // TCH)
        sl = slice(gch * TCH, (gch + 1) * TCH)
        kn2b = C.bank(hold=True)
        for j in range(NT):
            hm = C.hmask[:, j * H:(j + 1) * H]
            bk = C.bank()
            p.mm(bk[:, 0:TCH], [(wuk[kc][:, j * 128:(j + 1) * 128], latent[kc][:, sl]) for kc in range(2)],
                 tag=f"l1a_rk{gch}_{j}")
            p.act(stage2[0][:], bk[:, 0:TCH], AF.Copy, tag=f"l1a_rke{gch}_{j}")
            p.dma("sync", k_scr[j * 128:(j + 1) * 128, sl], stage2[0][:],
                  reads=[stage2[0]], writes=[f"{nm(k_scr)}#t{j}"], tag=f"l1a_rks{gch}_{j}")
            p.act(sqs[:], stage2[0][:], AF.Square, tag=f"l1a_sqk{gch}_{j}")
            p.mm(kn2b[0:H, 0:TCH], [(hm, sqs[:])], start=(j == 0), stop=(j == NT - 1),
                 tag=f"l1a_kn{gch}_{j}")
            bk2 = C.bank()
            p.mm(bk2[:, 0:TCH], [(wuv[kc][:, j * 128:(j + 1) * 128], latent[kc][:, sl]) for kc in range(2)],
                 tag=f"l1a_rv{gch}_{j}")
            p.act(stage2[1][:], bk2[:, 0:TCH], AF.Copy, tag=f"l1a_rve{gch}_{j}")
            p.dma("sync", v_scr[j * 128:(j + 1) * 128, sl], stage2[1][:],
                  reads=[stage2[1]], writes=[f"{nm(v_scr)}#t{j}"], tag=f"l1a_rvs{gch}_{j}")
        p.reduce(C.scr[3][0:H, 0:1], kn2b[0:H, 0:TCH], ALU.max, tag=f"l1a_knm{gch}")
        p.tt(kmax2[:, b:b + 1], kmax2[:, b:b + 1], C.scr[3][0:H, 0:1], ALU.max,
             tag=f"l1a_knu{gch}")
        C.release(kn2b)
    for b in range(BPC):
        bsl = slice(b * S, (b + 1) * S)
        p.ts(qn2[:, bsl], qn2[:, bsl], kmax2[:, b:b + 1], None, ALU.mult, tag=f"l1a_mu{b}")
        p.act(qn2[:, bsl], qn2[:, bsl], AF.Sqrt, tag=f"l1a_mus{b}")
    p.barrier(tag="l1a_ph1c_end")
    p.free(*stage2[::-1], *wuv[::-1], *wuk[::-1])

    wo = load_w_ft(C, W["L1_attn_Wo"], D, D, "l1a_wo")
    qt = p.sbuf("l1a_qt", (128, S), BF16)
    ktt = p.sbuf("l1a_kt", (128, S), BF16)
    vt = p.sbuf("l1a_vt", (128, S), BF16)
    v_tm = p.sbuf("l1a_vtm", (128, S), BF16)
    pt_pool = [p.sbuf(f"l1a_pt{i}", (128, TCH), BF16) for i in range(2)]
    srow = p.sbuf("l1a_srow", (1, TCH), F32)
    negm_row = p.sbuf("l1a_negm", (1, S), BF16)
    rsum_sb = p.sbuf("l1a_rsum", (128, TCH), F32)
    attn_out = [p.sbuf(f"l1a_ao{i}", (128, S), BF16) for i in range(NT)]
    decay_neg = p.sbuf("l1a_decay", (128, 4608), BF16)
    p.dma("sync", decay_neg[:], W["L1_decay_neg"], writes=[decay_neg], tag="l1a_dld")

    for b in range(BPC):
        attn_core(C, b, H, DH, q_scr, k_scr, v_scr, qn2, kmax2,
                  attn_out, srow, negm_row, qt, ktt, vt, v_tm, pt_pool,
                  rsum_sb, decay_neg=decay_neg, tagp="l1")
        for ch in range(S // TCH):
            gch = b * (S // TCH) + ch
            sl = slice(gch * TCH, (gch + 1) * TCH)
            csl = slice(ch * TCH, (ch + 1) * TCH)
            for j in range(NT):
                bk = C.bank()
                p.mm(bk[:, 0:TCH],
                     [(wo[kc][:, j * 128:(j + 1) * 128], attn_out[kc][:, csl]) for kc in range(NT)],
                     tag=f"l1a_o{gch}_{j}")
                rb = rb_pool[j % 2]
                p.dma("sync", rb[:], r_in[j * 128:(j + 1) * 128, sl],
                      reads=[rtile(r_in, j)], writes=[rb], tag=f"l1a_tl{gch}_{j}")
                s = C.scr[j % 2]
                p.tt(s[:, 0:TCH], rb[:], C.abc[:, sl], ALU.mult, tag=f"l1a_t1{gch}_{j}")
                p.tt(s[:, 0:TCH], s[:, 0:TCH], C.bbc[:, sl], ALU.add, tag=f"l1a_t2{gch}_{j}")
                ro = robuf[j % 2]
                p.tt(ro[:], s[:, 0:TCH], bk[:, 0:TCH], ALU.add, tag=f"l1a_t3{gch}_{j}")
                C.tail_chunk(j, gch, ro[:])
                p.dma("sync", r_out[j * 128:(j + 1) * 128, sl], ro[:], reads=[ro],
                      writes=[rtile(r_out, j)], store=True, tag=f"l1a_st{gch}_{j}")
    C.tail_finish()


def emit_L2_ffn(C, W, r_in, r_out):
    p = C.p
    nc = p.nc
    H, DH = 8, 128
    ECNT = 4
    SC = 1.0 / 32.0
    EPS = 1e-5
    q_scr = nc.dram_tensor("l2f_qscr", [D, T], BF16, kind="Internal").ap()
    k_scr = nc.dram_tensor("l2f_kscr", [D, T], BF16, kind="Internal").ap()
    v_scr = nc.dram_tensor("l2f_vscr", [D, T], BF16, kind="Internal").ap()

    C.rows_to_bcast()
    cbc = p.sbuf("l2f_cbc", (128, T), F32)
    s0 = C.rscr[0:1, 0:TCH]
    for c in range(NCH):
        sl = slice(c * TCH, (c + 1) * TCH)
        p.act(s0, C.rA[0:1, sl], AF.Square, tag=f"l2f_c2{c}")
        p.ts(s0, s0, -EPS, 1.0 + EPS, ALU.mult, ALU.add, tag=f"l2f_c3{c}")
        p.act(s0, s0, AF.Sqrt, tag=f"l2f_c4{c}")
        p.recip(s0, s0, tag=f"l2f_c5{c}")
        bk = C.bank()
        p.mm(bk[:, 0:TCH], [(C.onec1[:], s0)], tag=f"l2f_cb{c}")
        p.copy(cbc[:, sl], bk[:, 0:TCH], tag=f"l2f_cbe{c}")
    C.stats_reset()

    qn2 = p.sbuf("l2f_qn2", (H, T), F32)
    kmax2 = p.sbuf("l2f_kmax2", (H, BPC), F32)
    rb_pool = [p.sbuf(f"l2f_rb{i}", (128, TCH), F32) for i in range(2)]
    robuf = [p.sbuf(f"l2f_ro{i}", (128, TCH), F32) for i in range(2)]
    nw = load_vec_cols(C, W["L2_ffn_norm_w"], D, "l2f_nw")
    nb = load_vec_cols(C, W["L2_ffn_norm_b"], D, "l2f_nb")
    sqs = p.sbuf("l2f_sqs", (128, TCH), F32)
    p.memset(kmax2[:], 0.0, tag="l2f_km0")
    biases = {}
    for bn in ("q_b", "k_b", "v_b", "o_b"):
        t = p.sbuf(f"l2f_{bn}", (1, D), BF16)
        p.dma("gpsimd", t[:], W[f"L2_ffn_{bn}"].unsqueeze(0), writes=[t],
              tag=f"l2f_bld_{bn}")
        biases[bn] = t
    onerb = p.sbuf("l2f_onerb", (1, TCH), BF16)
    p.dma("sync", onerb[:], C.consts["ones_row_bf"], writes=[onerb], tag="l2f_onerb")

    wq = load_w_ft(C, W["L2_ffn_q_w_T"], D, D, "l2f_wq")
    wk = load_w_ft(C, W["L2_ffn_k_w_T"], D, D, "l2f_wk")
    wv = load_w_ft(C, W["L2_ffn_v_w_T"], D, D, "l2f_wv")
    xbf_c = [p.sbuf(f"l2f_xc{i}", (128, TCH), BF16) for i in range(NT)]
    stage = [p.sbuf(f"l2f_stg{i}", (128, TCH), BF16) for i in range(3)]

    for gch in range(NCH):
        b, e = divmod(gch, ECNT)
        m_t = (D >> e) // 128
        sl = slice(gch * TCH, (gch + 1) * TCH)
        for kc in range(m_t):
            rb = rb_pool[kc % 2]
            p.dma("sync", rb[:], r_in[kc * 128:(kc + 1) * 128, sl],
                  reads=[rtile(r_in, kc)], writes=[rb], tag=f"l2f_ld{gch}_{kc}")
            s = C.scr[kc % 2]
            p.tt(s[:, 0:TCH], rb[:], C.abc[:, sl], ALU.mult, tag=f"l2f_m{gch}_{kc}")
            p.tt(s[:, 0:TCH], s[:, 0:TCH], C.bbc[:, sl], ALU.add, tag=f"l2f_a{gch}_{kc}")
            p.tt(s[:, 0:TCH], s[:, 0:TCH], cbc[:, sl], ALU.mult, tag=f"l2f_cn{gch}_{kc}")
            p.ts(xbf_c[kc][:], s[:, 0:TCH], nw[:, kc:kc + 1], nb[:, kc:kc + 1],
                 ALU.mult, ALU.add, tag=f"l2f_wn{gch}_{kc}")
        qn2b = C.bank(hold=True)
        kn2b = C.bank(hold=True)
        for j in range(NT):
            hm = C.hmask8[:, j * H:(j + 1) * H]
            for wt, bkey, scr_, st_i, scale in ((wq, "q_b", q_scr, 0, SC), (wk, "k_b", k_scr, 1, 1.0), (wv, "v_b", v_scr, 2, 1.0)):
                bk = C.bank()
                pairs = [(wt[kc][:, j * 128:(j + 1) * 128], xbf_c[kc][:]) for kc in range(m_t)]
                pairs.append((biases[bkey][0:1, j * 128:(j + 1) * 128], onerb[:]))
                p.mm(bk[:, 0:TCH], pairs, tag=f"l2f_p{gch}_{j}_{st_i}")
                stg = stage[st_i]
                p.act(stg[:], bk[:, 0:TCH], AF.Copy, scale=scale, tag=f"l2f_pe{gch}_{j}_{st_i}")
                p.dma("sync", scr_[j * 128:(j + 1) * 128, sl], stg[:],
                      reads=[stg], writes=[f"{nm(scr_)}#t{j}"], tag=f"l2f_ps{gch}_{j}_{st_i}")
                if st_i == 0:
                    p.act(sqs[:], stg[:], AF.Square, tag=f"l2f_sq{gch}_{j}")
                    p.mm(qn2b[0:H, 0:TCH], [(hm, sqs[:])], start=(j == 0), stop=(j == NT - 1),
                         tag=f"l2f_qn{gch}_{j}")
                elif st_i == 1:
                    p.act(sqs[:], stg[:], AF.Square, tag=f"l2f_sqk{gch}_{j}")
                    p.mm(kn2b[0:H, 0:TCH], [(hm, sqs[:])], start=(j == 0), stop=(j == NT - 1),
                         tag=f"l2f_kn{gch}_{j}")
        p.copy(qn2[:, sl], qn2b[0:H, 0:TCH], tag=f"l2f_qne{gch}")
        p.reduce(C.scr[3][0:H, 0:1], kn2b[0:H, 0:TCH], ALU.max, tag=f"l2f_knm{gch}")
        p.tt(kmax2[:, b:b + 1], kmax2[:, b:b + 1], C.scr[3][0:H, 0:1], ALU.max,
             tag=f"l2f_knu{gch}")
        C.release(qn2b, kn2b)
    for b in range(BPC):
        bsl = slice(b * S, (b + 1) * S)
        p.ts(qn2[:, bsl], qn2[:, bsl], kmax2[:, b:b + 1], None, ALU.mult, tag=f"l2f_mu{b}")
        p.act(qn2[:, bsl], qn2[:, bsl], AF.Sqrt, tag=f"l2f_mus{b}")
    p.barrier(tag="l2f_ph1_end")
    p.free(*stage[::-1], *xbf_c[::-1], *wv[::-1], *wk[::-1], *wq[::-1])

    wo = load_w_ft(C, W["L2_ffn_o_w_T"], D, D, "l2f_wo")
    qt = p.sbuf("l2f_qt", (128, S), BF16)
    ktt = p.sbuf("l2f_kt", (128, S), BF16)
    vt = p.sbuf("l2f_vt", (128, S), BF16)
    v_tm = p.sbuf("l2f_vtm", (128, S), BF16)
    pt_pool = [p.sbuf(f"l2f_pt{i}", (128, TCH), BF16) for i in range(2)]
    srow = p.sbuf("l2f_srow", (1, TCH), F32)
    negm_row = p.sbuf("l2f_negm", (1, S), BF16)
    rsum_sb = p.sbuf("l2f_rsum", (128, TCH), F32)
    attn_out = [p.sbuf(f"l2f_ao{i}", (128, S), BF16) for i in range(NT)]

    for b in range(BPC):
        attn_core(C, b, H, DH, q_scr, k_scr, v_scr, qn2, kmax2,
                  attn_out, srow, negm_row, qt, ktt, vt, v_tm, pt_pool,
                  rsum_sb, tagp="l2f")
        for ch in range(S // TCH):
            e = ch
            m_t = (D >> e) // 128
            gch = b * (S // TCH) + ch
            sl = slice(gch * TCH, (gch + 1) * TCH)
            csl = slice(ch * TCH, (ch + 1) * TCH)
            for j in range(NT):
                rb = rb_pool[j % 2]
                p.dma("sync", rb[:], r_in[j * 128:(j + 1) * 128, sl],
                      reads=[rtile(r_in, j)], writes=[rb], tag=f"l2f_tl{gch}_{j}")
                s = C.scr[j % 2]
                p.tt(s[:, 0:TCH], rb[:], C.abc[:, sl], ALU.mult, tag=f"l2f_t1{gch}_{j}")
                p.tt(s[:, 0:TCH], s[:, 0:TCH], C.bbc[:, sl], ALU.add, tag=f"l2f_t2{gch}_{j}")
                ro = robuf[j % 2]
                if j < m_t:
                    bk = C.bank()
                    pairs = [(wo[kc][:, j * 128:(j + 1) * 128], attn_out[kc][:, csl])
                             for kc in range(m_t)]
                    pairs.append((biases["o_b"][0:1, j * 128:(j + 1) * 128], onerb[:]))
                    p.mm(bk[:, 0:TCH], pairs, tag=f"l2f_o{gch}_{j}")
                    p.tt(ro[:], s[:, 0:TCH], s[:, 0:TCH], ALU.add, tag=f"l2f_t3{gch}_{j}")
                    p.tt(ro[:], ro[:], bk[:, 0:TCH], ALU.add, tag=f"l2f_t4{gch}_{j}")
                else:
                    p.tt(ro[:], s[:, 0:TCH], s[:, 0:TCH], ALU.add, tag=f"l2f_t3{gch}_{j}")
                C.tail_chunk(j, gch, ro[:])
                p.dma("sync", r_out[j * 128:(j + 1) * 128, sl], ro[:], reads=[ro],
                      writes=[rtile(r_out, j)], store=True, tag=f"l2f_st{gch}_{j}")
    C.tail_finish()


def _router_topk(C, p, tt, sc, wsm, msk, tnum, bn_bc, eyef, ctstage, cw_scr):
    N_E = 8
    p.reduce(tnum[:, 0:1], sc[:], ALU.max, tag=f"l0f_smx{tt}")
    p.ts(tnum[:, 1:2], tnum[:, 0:1], -1.0, None, ALU.mult, tag=f"l0f_smn{tt}")
    p.act(wsm[:], sc[:], AF.Exp, bias=tnum[:, 1:2], accum_out=tnum[:, 2:3], tag=f"l0f_se{tt}")
    p.recip(tnum[:, 2:3], tnum[:, 2:3], tag=f"l0f_sr{tt}")
    p.ts(wsm[:], wsm[:], tnum[:, 2:3], None, ALU.mult, tag=f"l0f_sw{tt}")
    p.tt(sc[:], sc[:], bn_bc[:], ALU.add, tag=f"l0f_tb{tt}")
    p.reduce(tnum[:, 0:1], sc[:], ALU.max, tag=f"l0f_t1{tt}")
    p.ts(msk[:], sc[:], tnum[:, 0:1], None, ALU.is_ge, tag=f"l0f_tm1{tt}")
    p.stt(sc[:], msk[:], -1e30, sc[:], ALU.mult, ALU.add, tag=f"l0f_t2{tt}")
    p.reduce(tnum[:, 0:1], sc[:], ALU.max, tag=f"l0f_t3{tt}")
    p.ts(sc[:], sc[:], tnum[:, 0:1], None, ALU.is_ge, tag=f"l0f_tm2{tt}")
    p.tt(msk[:], msk[:], sc[:], ALU.add, tag=f"l0f_tm{tt}")
    p.tt(wsm[:], wsm[:], msk[:], ALU.mult, tag=f"l0f_cw1{tt}")
    p.reduce(tnum[:, 0:1], wsm[:], ALU.add, tag=f"l0f_cd{tt}")
    p.recip(tnum[:, 0:1], tnum[:, 0:1], tag=f"l0f_cdr{tt}")
    p.ts(wsm[:], wsm[:], tnum[:, 0:1], None, ALU.mult, tag=f"l0f_cw{tt}")
    bk2 = C.bank()
    p.transpose(bk2[0:N_E, 0:128], wsm[:], eyef[:], tag=f"l0f_ct{tt}")
    p.copy(ctstage[:], bk2[0:N_E, 0:128], tag=f"l0f_cte{tt}")
    p.dma("sync", cw_scr[:, tt * 128:(tt + 1) * 128], ctstage[:],
          reads=[ctstage], writes=["cwscr"], tag=f"l0f_ctw{tt}")


def emit_L0_ffn(C, W, r_in, r_out):
    p = C.p
    nc = p.nc
    N_E, HH = 8, 2048
    y_scr = nc.dram_tensor("l0f_yscr", [D, T], F32, kind="Internal").ap()
    act_scr = nc.dram_tensor("l0f_actscr", [HH, T], BF16, kind="Internal").ap()
    cw_scr = nc.dram_tensor("l0f_cwscr", [N_E, T], F32, kind="Internal").ap()

    C.rows_to_bcast()
    C.stats_reset()

    onerb = p.sbuf("l0f_onerb", (1, TCH), BF16)
    p.dma("sync", onerb[:], C.consts["ones_row_bf"], writes=[onerb], tag="l0f_onerb")
    oner128b = p.sbuf("l0f_oner128b", (1, 128), BF16)
    p.dma("sync", oner128b[:], C.consts["ones_r128_bf"], writes=[oner128b], tag="l0f_or128")
    eyef = p.sbuf("l0f_eyef", (128, 128), F32)
    p.dma("sync", eyef[:], C.consts["eye_f32"], writes=[eyef], tag="l0f_eyef")
    rb_pool = [p.sbuf(f"l0f_rb{i}", (128, TCH), F32) for i in range(2)]
    robuf = [p.sbuf(f"l0f_ro{i}", (128, TCH), F32) for i in range(2)]

    xbf = [p.sbuf(f"l0f_xbf{i}", (128, T), BF16) for i in range(NT)]

    wrTf = p.sbuf("l0f_wrTf", (128, N_E * NT), F32)
    for kc in range(NT):
        p.dma("sync", wrTf[:, kc * N_E:(kc + 1) * N_E],
              W["L0_ffn_Wr_T"][kc * 128:(kc + 1) * 128, :], writes=[wrTf], tag=f"l0f_wr{kc}")
    br_row = p.sbuf("l0f_br", (1, N_E), F32)
    p.dma("sync", br_row[:], W["L0_ffn_br"].unsqueeze(0), writes=[br_row], tag="l0f_brld")
    bn_row = p.sbuf("l0f_bn", (1, N_E), F32)
    p.dma("sync", bn_row[:], W["L0_ffn_biases_N"].unsqueeze(0), writes=[bn_row], tag="l0f_bnld")
    bn_bc = p.sbuf("l0f_bnbc", (128, N_E), F32)
    bk = C.bank()
    p.mm(bk[:, 0:N_E], [(C.onec1[:], bn_row[:])], tag="l0f_bnbc")
    p.copy(bn_bc[:], bk[:, 0:N_E], tag="l0f_bnbce")

    sc = p.sbuf("l0f_sc", (128, N_E), F32)
    wsm = p.sbuf("l0f_wsm", (128, N_E), F32)
    msk = p.sbuf("l0f_msk", (128, N_E), F32)
    tnum = p.sbuf("l0f_tnum", (128, 4), F32)
    ctstage = p.sbuf("l0f_ctst", (N_E, 128), F32)
    xs_pool = [p.sbuf(f"l0f_xs{i}", (128, TCH), F32) for i in range(2)]

    for gch in range(NCH):
        sl = slice(gch * TCH, (gch + 1) * TCH)
        rbanks = C.bank(4, hold=True)
        for kc in range(NT):
            rb = rb_pool[kc % 2]
            p.dma("sync", rb[:], r_in[kc * 128:(kc + 1) * 128, sl],
                  reads=[rtile(r_in, kc)], writes=[rb], tag=f"l0f_xl{gch}_{kc}")
            s = xs_pool[kc % 2]
            p.tt(s[:, 0:TCH], rb[:], C.abc[:, sl], ALU.mult, tag=f"l0f_xm{gch}_{kc}")
            p.tt(s[:, 0:TCH], s[:, 0:TCH], C.bbc[:, sl], ALU.add, tag=f"l0f_xb{gch}_{kc}")
            p.copy(xbf[kc][:, sl], s[:, 0:TCH], tag=f"l0f_xa{gch}_{kc}")
            for tb in range(TCH // 128):
                pairs = [(s[:, tb * 128:(tb + 1) * 128], wrTf[:, kc * N_E:(kc + 1) * N_E])]
                if kc == NT - 1:
                    pairs.append((C.onec1[:], br_row[:]))
                p.mm(rbanks[tb][0:128, 0:N_E], pairs, start=(kc == 0), stop=(kc == NT - 1),
                     tag=f"l0f_r{gch}_{kc}_{tb}")
        for tb in range(TCH // 128):
            tt = gch * 4 + tb
            p.copy(sc[:], rbanks[tb][0:128, 0:N_E], tag=f"l0f_re{tt}")
            _router_topk(C, p, tt, sc, wsm, msk, tnum, bn_bc, eyef, ctstage, cw_scr)
        C.release(*rbanks)

    p.barrier(tag="l0f_router_end")
    p.free(*xs_pool[::-1], ctstage, tnum, msk, wsm, sc, bn_bc, bn_row, br_row, wrTf)
    ys = [p.sbuf(f"l0f_ys{i}", (128, TCH), F32) for i in range(2)]
    w1p = [p.sbuf(f"l0f_w1p{i}", (128, 2 * 128 * NT), BF16) for i in range(2)]
    wl2h = p.sbuf("l0f_wl2h", (128, 16 * TCH), BF16)
    cwch = p.sbuf("l0f_cwch", (1, TCH), BF16)
    cwb_sb = p.sbuf("l0f_cwbsb", (128, TCH), F32)
    acti = [p.sbuf(f"l0f_ai{i}", (128, TCH), BF16) for i in range(16)]
    astg = p.sbuf("l0f_astg", (128, TCH), BF16)
    g1 = p.sbuf("l0f_g1", (128, TCH), F32)
    b1s = [p.sbuf(f"l0f_b1s{i}", (1, 256), BF16) for i in range(2)]

    for n in range(N_E):
        for jp in range(16):
            wb = w1p[jp % 2]
            wbv = wb[:].rearrange("p (k c) -> p k c", k=NT * 2)
            p.dma("gpsimd", wbv[:, 0:NT, :],
                  W["L0_ffn_Wl1"][n][:, jp * 128:(jp + 1) * 128]
                  .rearrange("(k p) c -> p k c", p=128),
                  writes=[wb], tag=f"l0f_w1a{n}_{jp}")
            p.dma("gpsimd", wbv[:, NT:2 * NT, :],
                  W["L0_ffn_Wl1"][n][:, HH + jp * 128:HH + (jp + 1) * 128]
                  .rearrange("(k p) c -> p k c", p=128),
                  writes=[wb], tag=f"l0f_w1b{n}_{jp}")
            bs = b1s[jp % 2]
            p.dma("gpsimd", bs[0:1, 0:128], W["L0_ffn_bl1"][n:n + 1, jp * 128:(jp + 1) * 128],
                  writes=[bs], tag=f"l0f_b1a{n}_{jp}")
            p.dma("gpsimd", bs[0:1, 128:256], W["L0_ffn_bl1"][n:n + 1, HH + jp * 128:HH + (jp + 1) * 128],
                  writes=[bs], tag=f"l0f_b1b{n}_{jp}")
            for ch in range(NCH):
                sl = slice(ch * TCH, (ch + 1) * TCH)
                ba = C.bank()
                p.mm(ba[:, 0:TCH],
                     [(wb[:, kc * 128:(kc + 1) * 128], xbf[kc][:, sl]) for kc in range(NT)]
                     + [(bs[0:1, 0:128], onerb[:])],
                     tag=f"l0f_ma{n}_{jp}_{ch}")
                bb = C.bank()
                p.mm(bb[:, 0:TCH],
                     [(wb[:, (NT + kc) * 128:(NT + kc + 1) * 128], xbf[kc][:, sl]) for kc in range(NT)]
                     + [(bs[0:1, 128:256], onerb[:])],
                     tag=f"l0f_mb{n}_{jp}_{ch}")
                p.act(g1[:], ba[:, 0:TCH], AF.Silu, tag=f"l0f_si{n}_{jp}_{ch}")
                p.tt(astg[:], g1[:], bb[:, 0:TCH], ALU.mult, tag=f"l0f_gb{n}_{jp}_{ch}")
                p.dma("sync", act_scr[jp * 128:(jp + 1) * 128, sl], astg[:],
                      reads=[astg], writes=[f"actscr#{jp}"], tag=f"l0f_as{n}_{jp}_{ch}")
        for half in range(2):
            for _k0 in range(0, 16, 8):
                p.op("gpsimd", (lambda e, k0=_k0, n=n, half=half: e.dma_start(
                    out=wl2h[:].rearrange("p (k c) -> p k c", k=16)[:, k0:k0 + 8, :],
                    in_=W["L0_ffn_Wl2"][n][:, half * TCH:(half + 1) * TCH]
                    .rearrange("(k p) c -> p k c", p=128)[:, k0:k0 + 8, :])),
                    reads=[], writes=[wl2h], dma=True, tag=f"l0f_w2l{n}_{half}_{_k0}")
            for ch in range(NCH):
                sl = slice(ch * TCH, (ch + 1) * TCH)
                for ka in range(16):
                    p.dma("sync", acti[ka][:], act_scr[ka * 128:(ka + 1) * 128, sl],
                          reads=[f"actscr#{ka}"], writes=[acti[ka]], tag=f"l0f_al{n}_{half}_{ch}_{ka}")
                p.dma("gpsimd", cwch[:], cw_scr[n:n + 1, sl], reads=["cwscr"],
                      writes=[cwch], tag=f"l0f_cwc{n}_{half}_{ch}")
                bcw = C.bank()
                p.mm(bcw[:, 0:TCH], [(oner128b[:], cwch[:])], tag=f"l0f_cb{n}_{half}_{ch}")
                p.copy(cwb_sb[:], bcw[:, 0:TCH], tag=f"l0f_cbe{n}_{half}_{ch}")
                for jo in range(4):
                    j = half * 4 + jo
                    by = C.bank()
                    p.mm(by[:, 0:TCH],
                         [(wl2h[:, (ka * 4 + jo) * 128:(ka * 4 + jo + 1) * 128], acti[ka][:])
                          for ka in range(16)],
                         tag=f"l0f_m2{n}_{j}_{ch}")
                    yst = ys[jo % 2]
                    p.tt(yst[:], by[:, 0:TCH], cwb_sb[:], ALU.mult, tag=f"l0f_ye{n}_{j}_{ch}")
                    p.dma("gpsimd", y_scr[j * 128:(j + 1) * 128, sl], yst[:],
                          reads=[yst], writes=[f"yscr#{j}"], tag=f"l0f_ya{n}_{j}_{ch}",
                          **({} if n == 0 else {"accum_op": ALU.add}))
    p.barrier(tag="l0f_exp_end")
    p.free(*b1s[::-1], g1, astg, *acti[::-1], cwb_sb, cwch, wl2h, *w1p[::-1])
    bl2b = p.sbuf("l0f_bl2", (N_E, D), BF16)
    p.dma("gpsimd", bl2b[:], W["L0_ffn_bl2"], writes=[bl2b], tag="l0f_bl2ld")
    cwt = p.sbuf("l0f_cwt", (N_E, TCH), BF16)
    for gch in range(NCH):
        sl = slice(gch * TCH, (gch + 1) * TCH)
        p.dma("gpsimd", cwt[:], cw_scr[:, sl], reads=["cwscr"], writes=[cwt], tag=f"l0f_cwt{gch}")
        for j in range(NT):
            bk = C.bank()
            p.mm(bk[:, 0:TCH], [(bl2b[:, j * 128:(j + 1) * 128], cwt[:])],
                 tag=f"l0f_b2{gch}_{j}")
            rb = rb_pool[j % 2]
            p.dma("sync", rb[:], r_in[j * 128:(j + 1) * 128, sl],
                  reads=[rtile(r_in, j)], writes=[rb], tag=f"l0f_tl{gch}_{j}")
            s = C.scr[j % 2]
            p.tt(s[:, 0:TCH], rb[:], C.abc[:, sl], ALU.mult, tag=f"l0f_t1{gch}_{j}")
            p.tt(s[:, 0:TCH], s[:, 0:TCH], C.bbc[:, sl], ALU.add, tag=f"l0f_t2{gch}_{j}")
            ro = robuf[j % 2]
            yld = ys[j % 2]
            p.dma("sync", yld[:], y_scr[j * 128:(j + 1) * 128, sl],
                  reads=[f"yscr#{j}"], writes=[yld], tag=f"l0f_yl{gch}_{j}")
            p.tt(s[:, 0:TCH], s[:, 0:TCH], yld[:], ALU.add, tag=f"l0f_t3{gch}_{j}")
            p.tt(ro[:], s[:, 0:TCH], bk[:, 0:TCH], ALU.add, tag=f"l0f_t4{gch}_{j}")
            C.tail_chunk(j, gch, ro[:])
            p.dma("sync", r_out[j * 128:(j + 1) * 128, sl], ro[:], reads=[ro],
                  writes=[rtile(r_out, j)], store=True, tag=f"l0f_st{gch}_{j}")
    C.tail_finish()



def emit_L3_ffn(C, W, r_in, r_out):
    p = C.p
    nc = p.nc
    NH, NG, NCC = 4, 8, 4
    z_scr = nc.dram_tensor("l3f_zscr", [16384, T], BF16, kind="Internal").ap()
    wc_bf = nc.dram_tensor("l3f_wcbf", [16384, D], BF16, kind="Internal").ap()

    C.rows_to_bcast()
    C.stats_reset()

    rb_pool = [p.sbuf(f"l3f_rb{i}", (128, TCH), F32) for i in range(2)]
    robuf = [p.sbuf(f"l3f_ro{i}", (128, TCH), F32) for i in range(2)]

    for i in range(16):
        p.dma("gpsimd", wc_bf[i * 1024:(i + 1) * 1024, :],
              W["L3_ffn_w_cproj_T"][i * 1024:(i + 1) * 1024, :],
              writes=[f"wcbf#{i % 4}"], tag=f"l3f_wc{i}")

    xbf = [p.sbuf(f"l3f_xbf{i}", (128, T), BF16) for i in range(NT)]
    for gch in range(NCH):
        sl = slice(gch * TCH, (gch + 1) * TCH)
        for kc in range(NT):
            rb = rb_pool[kc % 2]
            p.dma("sync", rb[:], r_in[kc * 128:(kc + 1) * 128, sl],
                  reads=[rtile(r_in, kc)], writes=[rb], tag=f"l3f_xl{gch}_{kc}")
            s = C.scr[kc % 2]
            p.tt(s[:, 0:TCH], rb[:], C.abc[:, sl], ALU.mult, tag=f"l3f_xm{gch}_{kc}")
            p.tt(xbf[kc][:, sl], s[:, 0:TCH], C.bbc[:, sl], ALU.add, tag=f"l3f_xa{gch}_{kc}")
    bdm = p.sbuf("l3f_bdm", (128, NH * NG * 128), BF16)
    for _a0 in range(0, NH * NG, 8):
        p.op("gpsimd", (lambda e, a0=_a0: e.dma_start(
            out=bdm[:].rearrange("p (a c) -> p a c", a=NH * NG)[:, a0:a0 + 8, :],
            in_=W["L3_ffn_bd_maker"].rearrange("h g p c -> p (h g) c")[:, a0:a0 + 8, :],
            allow_slow_non_contiguous=True)),
            reads=[], writes=[bdm], dma=True, tag=f"l3f_bdm{_a0}")
    bdu = p.sbuf("l3f_bdu", (128, NH * NG * NCC * 128), BF16)
    _hsz = NG * NCC * 128
    for h in range(NH):
        for _a0 in range(0, NG * NCC, 8):
            p.op("gpsimd", (lambda e, h=h, a0=_a0: e.dma_start(
                out=bdu[:, h * _hsz:(h + 1) * _hsz].rearrange("p (a c) -> p a c", a=NG * NCC)[:, a0:a0 + 8, :],
                in_=W["L3_ffn_bd_up"][h].rearrange("g q p c -> p (g q) c")[:, a0:a0 + 8, :],
                allow_slow_non_contiguous=True)),
                reads=[], writes=[bdu], dma=True, tag=f"l3f_bdu{h}_{_a0}")
    yb = p.sbuf("l3f_yb", (128, TCH), BF16)
    zb = [p.sbuf(f"l3f_zb{i}", (128, TCH), BF16) for i in range(2)]
    for gch in range(NCH):
        sl = slice(gch * TCH, (gch + 1) * TCH)
        for h in range(NH):
            for g in range(NG):
                a = h * NG + g
                by = C.bank()
                p.mm(by[:, 0:TCH], [(bdm[:, a * 128:(a + 1) * 128], xbf[g][:, sl])],
                     tag=f"l3f_y{gch}_{a}")
                p.act(yb[:], by[:, 0:TCH], AF.Copy, tag=f"l3f_ye{gch}_{a}")
                for cc in range(NCC):
                    q = a * NCC + cc
                    bz = C.bank()
                    p.mm(bz[:, 0:TCH], [(bdu[:, q * 128:(q + 1) * 128], yb[:])],
                         tag=f"l3f_z{gch}_{q}")
                    zt = zb[cc % 2]
                    p.act(zt[:], bz[:, 0:TCH], AF.Gelu, tag=f"l3f_ze{gch}_{q}")
                    row0 = h * 4096 + g * 512 + cc * 128
                    p.dma("sync", z_scr[row0:row0 + 128, sl], zt[:],
                          reads=[zt], writes=[f"zscr#{q % 8}"], tag=f"l3f_zs{gch}_{q}")
    p.barrier(tag="l3f_rails_end")
    p.free(*zb[::-1], yb, bdu, bdm, *xbf[::-1])

    zp = [p.sbuf(f"l3f_zp{i}", (128, TCH), BF16) for i in range(16)]
    wp = [p.sbuf(f"l3f_wp{i}", (128, D), BF16) for i in range(16)]
    NKG = 8
    for gch in range(NCH):
        sl = slice(gch * TCH, (gch + 1) * TCH)
        for half in range(2):
            obanks = C.bank(4, hold=True)
            for kg in range(NKG):
                for i in range(16):
                    kt = kg * 16 + i
                    p.dma("sync", zp[i][:], z_scr[kt * 128:(kt + 1) * 128, sl],
                          reads=[f"zscr#{kt % 8}"], writes=[zp[i]], tag=f"l3f_zl{gch}_{half}_{kt}")
                    p.dma("gpsimd", wp[i][:], wc_bf[kt * 128:(kt + 1) * 128, :],
                          reads=[f"wcbf#{kt // 32}"], writes=[wp[i]], tag=f"l3f_wl{gch}_{half}_{kt}")
                for mo in range(4):
                    m = half * 4 + mo
                    p.mm(obanks[mo][:, 0:TCH],
                         [(wp[i][:, m * 128:(m + 1) * 128], zp[i][:]) for i in range(16)],
                         start=(kg == 0), stop=(kg == NKG - 1), tag=f"l3f_c{gch}_{m}_{kg}")
            for mo in range(4):
                m = half * 4 + mo
                rb = rb_pool[mo % 2]
                p.dma("sync", rb[:], r_in[m * 128:(m + 1) * 128, sl],
                      reads=[rtile(r_in, m)], writes=[rb], tag=f"l3f_tl{gch}_{m}")
                s = C.scr[mo % 2]
                p.tt(s[:, 0:TCH], rb[:], C.abc[:, sl], ALU.mult, tag=f"l3f_t1{gch}_{m}")
                p.tt(s[:, 0:TCH], s[:, 0:TCH], C.bbc[:, sl], ALU.add, tag=f"l3f_t2{gch}_{m}")
                ro = robuf[mo % 2]
                p.tt(ro[:], s[:, 0:TCH], obanks[mo][:, 0:TCH], ALU.add, tag=f"l3f_t3{gch}_{m}")
                C.tail_chunk(m, gch, ro[:])
                p.dma("sync", r_out[m * 128:(m + 1) * 128, sl], ro[:], reads=[ro],
                      writes=[rtile(r_out, m)], store=True, tag=f"l3f_st{gch}_{m}")
            C.release(*obanks)
    C.tail_finish()


def emit_finalize(C, r_src, x_out):
    p = C.p
    C.rows_to_bcast()
    rb_pool = [p.sbuf(f"fin_rb{i}", (128, TCH), F32) for i in range(2)]
    xo = [p.sbuf(f"fin_xo{i}", (128, TCH), F32) for i in range(2)]
    for gch in range(NCH):
        sl = slice(gch * TCH, (gch + 1) * TCH)
        for j in range(NT):
            rb = rb_pool[j % 2]
            p.dma("sync", rb[:], r_src[j * 128:(j + 1) * 128, sl],
                  reads=[rtile(r_src, j)], writes=[rb], tag=f"fin_l{gch}_{j}")
            s = C.scr[j % 2]
            p.tt(s[:, 0:TCH], rb[:], C.abc[:, sl], ALU.mult, tag=f"fin_m{gch}_{j}")
            x = xo[j % 2]
            p.tt(x[:], s[:, 0:TCH], C.bbc[:, sl], ALU.add, tag=f"fin_a{gch}_{j}")
            p.dma("sync", x_out[j * 128:(j + 1) * 128, sl], x[:], reads=[x],
                  writes=[f"{nm(x_out)}#t{j}"], store=True, tag=f"fin_s{gch}_{j}")



def emit_L2_attn(C, W, r_in, r_out):
    p = C.p
    nc = p.nc
    H, DH = 16, 64
    NSB = 16
    VW = 256
    EPS = 1e-5
    q_scr = nc.dram_tensor("l2a_qscr", [D, T], BF16, kind="Internal").ap()
    k_scr = nc.dram_tensor("l2a_kscr", [D, T], BF16, kind="Internal").ap()
    v_scr = nc.dram_tensor("l2a_vscr", [D, T], BF16, kind="Internal").ap()
    ao_scr = nc.dram_tensor("l2a_aoscr", [D, T], BF16, kind="Internal").ap()

    C.rows_to_bcast()
    C.stats_reset()

    rb_pool = [p.sbuf(f"l2a_rb{i}", (128, TCH), F32) for i in range(2)]
    robuf = [p.sbuf(f"l2a_ro{i}", (128, TCH), F32) for i in range(2)]
    lng = load_vec_cols(C, W["L2_attn_ln_g"], D, "l2a_lng")
    lnb = load_vec_cols(C, W["L2_attn_ln_b"], D, "l2a_lnb")
    biases = {}
    for bn in ("bq", "bk", "bv", "bo"):
        t = p.sbuf(f"l2a_{bn}", (1, D), BF16)
        p.dma("gpsimd", t[:], W[f"L2_attn_{bn}"].unsqueeze(0), writes=[t], tag=f"l2a_bld_{bn}")
        biases[bn] = t
    onerb = p.sbuf("l2a_onerb", (1, TCH), BF16)
    p.dma("sync", onerb[:], C.consts["ones_row_bf"], writes=[onerb], tag="l2a_onerb")
    hmaskb = p.sbuf("l2a_hmb", (128, 128), BF16)
    p.dma("gpsimd", hmaskb[:], C.consts["hmask16"], writes=[hmaskb], tag="l2a_hmb")

    cbc = p.sbuf("l2a_cbc", (128, T), F32)
    s0 = C.rscr[0:1, 0:TCH]
    for c in range(NCH):
        sl = slice(c * TCH, (c + 1) * TCH)
        p.act(s0, C.rA[0:1, sl], AF.Square, tag=f"l2a_c2{c}")
        p.ts(s0, s0, -EPS, 1.0 + EPS, ALU.mult, ALU.add, tag=f"l2a_c3{c}")
        p.act(s0, s0, AF.Sqrt, tag=f"l2a_c4{c}")
        p.recip(s0, s0, tag=f"l2a_c5{c}")
        bk = C.bank()
        p.mm(bk[:, 0:TCH], [(C.onec1[:], s0)], tag=f"l2a_cb{c}")
        p.copy(cbc[:, sl], bk[:, 0:TCH], tag=f"l2a_cbe{c}")

    wq = load_w_ft(C, W["L2_attn_Wq"], D, D, "l2a_wq")
    wk = load_w_ft(C, W["L2_attn_Wk"], D, D, "l2a_wk")
    wv = load_w_ft(C, W["L2_attn_Wv"], D, D, "l2a_wv")
    xbf_c = [p.sbuf(f"l2a_xc{i}", (128, TCH), BF16) for i in range(NT)]
    stage = [p.sbuf(f"l2a_stg{i}", (128, TCH), BF16) for i in range(3)]
    for gch in range(NCH):
        sl = slice(gch * TCH, (gch + 1) * TCH)
        for kc in range(NT):
            rb = rb_pool[kc % 2]
            p.dma("sync", rb[:], r_in[kc * 128:(kc + 1) * 128, sl],
                  reads=[rtile(r_in, kc)], writes=[rb], tag=f"l2a_ld{gch}_{kc}")
            s = C.scr[kc % 2]
            p.tt(s[:, 0:TCH], rb[:], C.abc[:, sl], ALU.mult, tag=f"l2a_m{gch}_{kc}")
            p.tt(s[:, 0:TCH], s[:, 0:TCH], C.bbc[:, sl], ALU.add, tag=f"l2a_a{gch}_{kc}")
            p.tt(s[:, 0:TCH], s[:, 0:TCH], cbc[:, sl], ALU.mult, tag=f"l2a_cn{gch}_{kc}")
            p.ts(xbf_c[kc][:], s[:, 0:TCH], lng[:, kc:kc + 1], lnb[:, kc:kc + 1],
                 ALU.mult, ALU.add, tag=f"l2a_wn{gch}_{kc}")
        for j in range(NT):
            for wt, bkey, scr_, st_i, scale in ((wq, "bq", q_scr, 0, 0.125), (wk, "bk", k_scr, 1, 1.0), (wv, "bv", v_scr, 2, 1.0)):
                bk = C.bank()
                pairs = [(wt[kc][:, j * 128:(j + 1) * 128], xbf_c[kc][:]) for kc in range(NT)]
                pairs.append((biases[bkey][0:1, j * 128:(j + 1) * 128], onerb[:]))
                p.mm(bk[:, 0:TCH], pairs, tag=f"l2a_p{gch}_{j}_{st_i}")
                stg = stage[st_i]
                p.act(stg[:], bk[:, 0:TCH], AF.Copy, scale=scale, tag=f"l2a_pe{gch}_{j}_{st_i}")
                p.dma("sync", scr_[j * 128:(j + 1) * 128, sl], stg[:],
                      reads=[stg], writes=[f"{nm(scr_)}#t{j}"], tag=f"l2a_ps{gch}_{j}_{st_i}")
    p.barrier(tag="l2a_ph1_end")
    p.free(*stage[::-1], *xbf_c[::-1], *wv[::-1], *wk[::-1], *wq[::-1], cbc)

    kfull = [p.sbuf(f"l2a_kf{i}", (128, T), BF16) for i in range(NT)]
    for j in range(NT):
        p.dma("sync", kfull[j][:], k_scr[j * 128:(j + 1) * 128, :],
              reads=[f"{nm(k_scr)}#t{j}"], writes=[kfull[j]], tag=f"l2a_kfl{j}")
    qch = [p.sbuf(f"l2a_qc{i}", (128, VW), BF16) for i in range(NT)]
    vch = [p.sbuf(f"l2a_vc{i}", (128, VW), BF16) for i in range(2)]
    prod = p.sbuf("l2a_prod", (128, VW), BF16)
    scs = p.sbuf("l2a_scs", (16, NSB * VW), F32)
    mxr = p.sbuf("l2a_mxr", (16, VW), F32)
    avs = p.sbuf("l2a_avs", (128, VW), F32)
    tsb = p.sbuf("l2a_tsb", (128, VW), F32)
    aos = p.sbuf("l2a_aos", (128, VW), BF16)
    wbf = p.sbuf("l2a_wbf", (16, NSB * VW), BF16)
    hmTb = p.sbuf("l2a_hmTb", (16, 128 * NT), BF16)
    p.dma("gpsimd", hmTb[:], C.consts["hmaskT16"], writes=[hmTb], tag="l2a_hmTb")

    for sb in range(NSB):
        sl = slice(sb * VW, (sb + 1) * VW)
        for j in range(NT):
            p.dma("sync", qch[j][:], q_scr[j * 128:(j + 1) * 128, sl],
                  reads=[f"{nm(q_scr)}#t{j}"], writes=[qch[j]], tag=f"l2a_ql{sb}_{j}")
        for pp in range(NSB):
            sb2 = (sb + pp + 1) % NSB
            ksl = slice(sb2 * VW, (sb2 + 1) * VW)
            bkp = C.bank(hold=True)
            for j in range(NT):
                p.tt(prod[:], qch[j][:], kfull[j][:, ksl], ALU.mult, tag=f"l2a_pr{sb}_{pp}_{j}")
                p.mm(bkp[0:16, 0:VW], [(hmaskb[:, j * 16:(j + 1) * 16], prod[:])],
                     start=(j == 0), stop=(j == NT - 1), tag=f"l2a_sc{sb}_{pp}_{j}")
            p.copy(scs[:, pp * VW:(pp + 1) * VW], bkp[0:16, 0:VW], tag=f"l2a_sce{sb}_{pp}")
            C.release(bkp)
        v3 = scs[:].rearrange("h (p v) -> h v p", p=NSB)
        p.reduce(mxr[0:16, :], v3, ALU.max, tag=f"l2a_mx{sb}")
        for pp in range(NSB):
            pl = scs[:, pp * VW:(pp + 1) * VW]
            p.tt(pl, pl, mxr[0:16, :], ALU.subtract, tag=f"l2a_sb{sb}_{pp}")
            p.act(pl, pl, AF.Exp, tag=f"l2a_ex{sb}_{pp}")
        p.reduce(mxr[0:16, :], v3, ALU.add, tag=f"l2a_sm{sb}")
        p.recip(mxr[0:16, :], mxr[0:16, :], tag=f"l2a_rs{sb}")
        for pp in range(NSB):
            pl = scs[:, pp * VW:(pp + 1) * VW]
            p.tt(wbf[:, pp * VW:(pp + 1) * VW], pl, mxr[0:16, :], ALU.mult,
                 tag=f"l2a_nw{sb}_{pp}")
        for j in range(NT):
            p.memset(avs[:], 0.0, tag=f"l2a_av0{sb}_{j}")
            for pp in range(NSB):
                sb2 = (sb + pp + 1) % NSB
                bkw = C.bank()
                p.mm(bkw[:, 0:VW], [(hmTb[:, j * 128:(j + 1) * 128],
                                     wbf[:, pp * VW:(pp + 1) * VW])], tag=f"l2a_wb{sb}_{j}_{pp}")
                vc = vch[pp % 2]
                p.dma("sync", vc[:], v_scr[j * 128:(j + 1) * 128, sb2 * VW:(sb2 + 1) * VW],
                      reads=[f"{nm(v_scr)}#t{j}"], writes=[vc], tag=f"l2a_vl{sb}_{j}_{pp}")
                p.tt(tsb[:], vc[:], bkw[:, 0:VW], ALU.mult, tag=f"l2a_vm{sb}_{j}_{pp}")
                p.tt(avs[:], avs[:], tsb[:], ALU.add, tag=f"l2a_va{sb}_{j}_{pp}")
            p.copy(aos[:], avs[:], tag=f"l2a_aoc{sb}_{j}")
            p.dma("sync", ao_scr[j * 128:(j + 1) * 128, sl], aos[:],
                  reads=[aos], writes=[f"aoscr#{j}"], tag=f"l2a_aos{sb}_{j}")
    p.barrier(tag="l2a_ph2_end")
    p.free(hmTb, wbf, aos, tsb, avs, mxr, scs, prod, *vch[::-1], *qch[::-1], *kfull[::-1])

    wo = load_w_ft(C, W["L2_attn_Wo"], D, D, "l2a_wo")
    aoc = [p.sbuf(f"l2a_aoc{i}", (128, TCH), BF16) for i in range(NT)]
    for gch in range(NCH):
        sl = slice(gch * TCH, (gch + 1) * TCH)
        for kc in range(NT):
            p.dma("sync", aoc[kc][:], ao_scr[kc * 128:(kc + 1) * 128, sl],
                  reads=[f"aoscr#{kc}"], writes=[aoc[kc]], tag=f"l2a_aol{gch}_{kc}")
        for j in range(NT):
            bk = C.bank()
            pairs = [(wo[kc][:, j * 128:(j + 1) * 128], aoc[kc][:]) for kc in range(NT)]
            pairs.append((biases["bo"][0:1, j * 128:(j + 1) * 128], onerb[:]))
            p.mm(bk[:, 0:TCH], pairs, tag=f"l2a_o{gch}_{j}")
            rb = rb_pool[j % 2]
            p.dma("sync", rb[:], r_in[j * 128:(j + 1) * 128, sl],
                  reads=[rtile(r_in, j)], writes=[rb], tag=f"l2a_tl{gch}_{j}")
            s = C.scr[j % 2]
            p.tt(s[:, 0:TCH], rb[:], C.abc[:, sl], ALU.mult, tag=f"l2a_t1{gch}_{j}")
            p.tt(s[:, 0:TCH], s[:, 0:TCH], C.bbc[:, sl], ALU.add, tag=f"l2a_t2{gch}_{j}")
            ro = robuf[j % 2]
            p.tt(ro[:], s[:, 0:TCH], s[:, 0:TCH], ALU.add, tag=f"l2a_t3{gch}_{j}")
            p.tt(ro[:], ro[:], bk[:, 0:TCH], ALU.add, tag=f"l2a_t4{gch}_{j}")
            C.tail_chunk(j, gch, ro[:])
            p.dma("sync", r_out[j * 128:(j + 1) * 128, sl], ro[:], reads=[ro],
                  writes=[rtile(r_out, j)], store=True, tag=f"l2a_st{gch}_{j}")
    C.tail_finish()



def emit_L3_attn(C, W, r_in, r_out):
    p = C.p
    nc = p.nc
    DI, DS, DTR = 2048, 16, 64
    NKT = DI // 128
    EPS = 1e-5
    xi_scr = nc.dram_tensor("l3a_xiscr", [DI, S], F32, kind="Internal").ap()
    z_scr = nc.dram_tensor("l3a_zscr", [DI, S], F32, kind="Internal").ap()
    xc_scr = nc.dram_tensor("l3a_xcscr", [DI, S], F32, kind="Internal").ap()
    xcb_scr = nc.dram_tensor("l3a_xcbscr", [DI, S], BF16, kind="Internal").ap()
    dt_scr = nc.dram_tensor("l3a_dtscr", [DI, S], BF16, kind="Internal").ap()
    dtu_scr = nc.dram_tensor("l3a_dtuscr", [DI, S], BF16, kind="Internal").ap()
    y_scr = nc.dram_tensor("l3a_yscr", [DI, S], BF16, kind="Internal").ap()
    outd_scr = nc.dram_tensor("l3a_outd", [2, D, S], F32, kind="Internal").ap()
    xmid_scr = nc.dram_tensor("l3a_xmid", [D, T], F32, kind="Internal").ap()

    C.rows_to_bcast()
    C.stats_reset()

    rb_pool = [p.sbuf(f"l3a_rb{i}", (128, TCH), F32) for i in range(2)]
    robuf = [p.sbuf(f"l3a_ro{i}", (128, TCH), F32) for i in range(2)]
    selJ = p.sbuf("l3a_selJ", (128, 2048), BF16)
    p.dma("sync", selJ[:], C.consts["l3_selJ"], writes=[selJ], tag="l3a_selJ")
    repB = p.sbuf("l3a_repB", (16, 128), BF16)
    p.dma("sync", repB[:], C.consts["l3_repB"], writes=[repB], tag="l3a_repB")
    onerb = p.sbuf("l3a_onerb", (1, TCH), BF16)
    p.dma("sync", onerb[:], C.consts["ones_row_bf"], writes=[onerb], tag="l3a_onerb")

    xch = [p.sbuf(f"l3a_xch{i}", (128, TCH), BF16) for i in range(NT)]
    wstr = [p.sbuf(f"l3a_wstr{i}", (128, 8 * 128), BF16) for i in range(2)]
    stg = [p.sbuf(f"l3a_stg{i}", (128, TCH), F32) for i in range(2)]
    stgb = [p.sbuf(f"l3a_stgb{i}", (128, TCH), BF16) for i in range(2)]

    def build_xchunk(b, ch, tag):
        gch = b * (S // TCH) + ch
        sl = slice(gch * TCH, (gch + 1) * TCH)
        for kc in range(NT):
            rb = rb_pool[kc % 2]
            p.dma("sync", rb[:], r_in[kc * 128:(kc + 1) * 128, sl],
                  reads=[rtile(r_in, kc)], writes=[rb], tag=f"{tag}_ld{gch}_{kc}")
            s = C.scr[kc % 2]
            p.tt(s[:, 0:TCH], rb[:], C.abc[:, sl], ALU.mult, tag=f"{tag}_m{gch}_{kc}")
            p.tt(xch[kc][:], s[:, 0:TCH], C.bbc[:, sl], ALU.add, tag=f"{tag}_a{gch}_{kc}")

    for b in range(BPC):
        for di, pref in enumerate(("fwd", "bwd")):
            tg = f"l3a{b}{pref[0]}"
            rev = di == 1
            for ch in range(S // TCH):
                build_xchunk(b, ch, f"{tg}A{ch}")
                for j in range(32):
                    wb = wstr[j % 2]
                    p.dma("gpsimd", wb[:].rearrange("q (k c) -> q k c", k=NT),
                          W[f"L3_{pref}_in_w_T"][:, j * 128:(j + 1) * 128]
                          .rearrange("(k q) c -> q k c", q=128),
                          writes=[wb], tag=f"{tg}Aw{ch}_{j}")
                    bk = C.bank()
                    p.mm(bk[:, 0:TCH],
                         [(wb[:, kc * 128:(kc + 1) * 128], xch[kc][:]) for kc in range(NT)],
                         tag=f"{tg}Am{ch}_{j}")
                    s = stg[j % 2]
                    p.copy(s[:, 0:TCH], bk[:, 0:TCH], tag=f"{tg}Ae{ch}_{j}")
                    dst = xi_scr if j < 16 else z_scr
                    jj = j % 16
                    p.dma("sync", dst[jj * 128:(jj + 1) * 128, ch * TCH:(ch + 1) * TCH],
                          s[:, 0:TCH], reads=[s], writes=[f"{nm(dst)}#t{jj}"], tag=f"{tg}As{ch}_{j}")
            cwc = p.sbuf(f"{tg}_cwc", (128, NKT * 4), F32)
            p.dma("sync", cwc[:].rearrange("q (k c) -> q k c", k=NKT),
                  W[f"L3_{pref}_conv_w"].rearrange("(k q) c -> q k c", q=128),
                  writes=[cwc], tag=f"{tg}Bw", allow_slow_non_contiguous=True)
            cbc_ = load_vec_cols(C, W[f"L3_{pref}_conv_b"], DI, f"{tg}_cb")
            xib = p.sbuf(f"{tg}_xib", (128, S + 6), F32)
            xco = p.sbuf(f"{tg}_xco", (128, S), F32)
            for kt in range(NKT):
                p.memset(xib[:, 0:3], 0.0, tag=f"{tg}Bz{kt}")
                p.memset(xib[:, S + 3:S + 6], 0.0, tag=f"{tg}Bz2{kt}")
                p.dma("sync", xib[:, 3:S + 3], xi_scr[kt * 128:(kt + 1) * 128, :],
                      reads=[f"{nm(xi_scr)}#t{kt}"], writes=[xib], tag=f"{tg}Bl{kt}")
                for j in range(4):
                    if not rev:
                        src = xib[:, j:j + S]
                    else:
                        src = xib[:, S + 5 - j:5 - j:-1]
                    if j == 0:
                        p.ts(xco[:], src, cwc[:, kt * 4:kt * 4 + 1], None, ALU.mult,
                             tag=f"{tg}Bc{kt}_{j}")
                    else:
                        p.stt(xco[:], src, cwc[:, kt * 4 + j:kt * 4 + j + 1], xco[:],
                              ALU.mult, ALU.add, tag=f"{tg}Bc{kt}_{j}")
                p.act(xco[:], xco[:], AF.Silu, bias=cbc_[:, kt:kt + 1], tag=f"{tg}Bs{kt}")
                p.dma("sync", xc_scr[kt * 128:(kt + 1) * 128, :], xco[:],
                      reads=[xco], writes=[f"{nm(xc_scr)}#t{kt}"], tag=f"{tg}Bst{kt}")
                bstg = stgb[kt % 2]
                for ch in range(S // TCH):
                    p.act(bstg[:], xco[:, ch * TCH:(ch + 1) * TCH], AF.Copy, tag=f"{tg}Bb{kt}_{ch}")
                    p.dma("sync", xcb_scr[kt * 128:(kt + 1) * 128, ch * TCH:(ch + 1) * TCH],
                          bstg[:], reads=[bstg], writes=[f"{nm(xcb_scr)}#t{kt}"], tag=f"{tg}Bbs{kt}_{ch}")
            p.barrier(tag=f"{tg}B_end")
            p.free(xco, xib, cbc_, cwc)
            xwT = p.sbuf(f"{tg}_xwT", (128, NKT * 96), BF16)
            for _k0 in range(0, NKT, 8):
                p.op("gpsimd", (lambda e, k0=_k0, xwT=xwT, pref=pref: e.dma_start(
                    out=xwT[:].rearrange("q (k c) -> q k c", k=NKT)[:, k0:k0 + 8, :],
                    in_=W[f"L3_{pref}_x_w_T"].rearrange("(k q) c -> q k c", q=128)[:, k0:k0 + 8, :],
                    allow_slow_non_contiguous=True)),
                    reads=[], writes=[xwT], dma=True, tag=f"{tg}Cw{_k0}")
            dtwT = p.sbuf(f"{tg}_dtwT", (64, DI), BF16)
            p.dma("gpsimd", dtwT[:], W[f"L3_{pref}_dt_w_T"], writes=[dtwT], tag=f"{tg}Cw2")
            dtb_c = load_vec_cols(C, W[f"L3_{pref}_dt_b"], DI, f"{tg}_dtb")
            xdbl_dt = p.sbuf(f"{tg}_xdbldt", (64, S), BF16)
            brow = p.sbuf(f"{tg}_brow", (16, S), BF16)
            crow = p.sbuf(f"{tg}_crow", (16, S), BF16)
            xcbt = [p.sbuf(f"{tg}_xcbt{i}", (128, TCH), BF16) for i in range(NKT)]
            for ch in range(S // TCH):
                csl = slice(ch * TCH, (ch + 1) * TCH)
                for kt in range(NKT):
                    p.dma("sync", xcbt[kt][:], xcb_scr[kt * 128:(kt + 1) * 128, csl],
                          reads=[f"{nm(xcb_scr)}#t{kt}"], writes=[xcbt[kt]], tag=f"{tg}Cl{ch}_{kt}")
                bk = C.bank()
                p.mm(bk[0:64, 0:TCH],
                     [(xwT[:, kt * 96:kt * 96 + 64], xcbt[kt][:]) for kt in range(NKT)],
                     tag=f"{tg}Cm{ch}")
                p.act(xdbl_dt[:, csl], bk[0:64, 0:TCH], AF.Copy, tag=f"{tg}Ce{ch}")
                bkb = C.bank()
                p.mm(bkb[0:16, 0:TCH],
                     [(xwT[:, kt * 96 + 64:kt * 96 + 80], xcbt[kt][:]) for kt in range(NKT)],
                     tag=f"{tg}Cmb{ch}")
                p.act(brow[:, csl], bkb[0:16, 0:TCH], AF.Copy, tag=f"{tg}Cb{ch}")
                bk2 = C.bank()
                p.mm(bk2[0:16, 0:TCH],
                     [(xwT[:, kt * 96 + 80:(kt + 1) * 96], xcbt[kt][:]) for kt in range(NKT)],
                     tag=f"{tg}Cm2{ch}")
                p.act(crow[:, csl], bk2[0:16, 0:TCH], AF.Copy, tag=f"{tg}Cc{ch}")
            sps = [p.sbuf(f"{tg}_sp{i}", (128, TCH), F32) for i in range(3)]
            for jt in range(NKT):
                for ch in range(S // TCH):
                    csl = slice(ch * TCH, (ch + 1) * TCH)
                    bk = C.bank()
                    p.mm(bk[:, 0:TCH],
                         [(dtwT[:, jt * 128:(jt + 1) * 128], xdbl_dt[:, csl])],
                         tag=f"{tg}Dm{jt}_{ch}")
                    x0 = sps[0]
                    p.ts(x0[:], bk[:, 0:TCH], dtb_c[:, jt:jt + 1], None, ALU.add, tag=f"{tg}Db{jt}_{ch}")
                    s1 = sps[1]
                    p.act(s1[:], x0[:], AF.Abs, tag=f"{tg}D1{jt}_{ch}")
                    p.act(s1[:], s1[:], AF.Exp, scale=-1.0, tag=f"{tg}D2{jt}_{ch}")
                    p.ts(s1[:], s1[:], 1.0, None, ALU.add, tag=f"{tg}D3{jt}_{ch}")
                    p.act(s1[:], s1[:], AF.Ln, tag=f"{tg}D4{jt}_{ch}")
                    s2 = sps[2]
                    p.ts(s2[:], x0[:], 0.0, None, ALU.max, tag=f"{tg}D5{jt}_{ch}")
                    p.tt(s2[:], s2[:], s1[:], ALU.add, tag=f"{tg}D6{jt}_{ch}")
                    bstg = stgb[0]
                    p.act(bstg[:], s2[:], AF.Copy, tag=f"{tg}D7{jt}_{ch}")
                    p.dma("sync", dt_scr[jt * 128:(jt + 1) * 128, csl], bstg[:],
                          reads=[bstg], writes=[f"{nm(dt_scr)}#t{jt}"], tag=f"{tg}D8{jt}_{ch}")
                    xcr = sps[0]
                    p.dma("sync", xcr[:], xc_scr[jt * 128:(jt + 1) * 128, csl],
                          reads=[f"{nm(xc_scr)}#t{jt}"], writes=[xcr], tag=f"{tg}D9{jt}_{ch}")
                    p.tt(s2[:], s2[:], xcr[:], ALU.mult, tag=f"{tg}Da{jt}_{ch}")
                    bstg2 = stgb[1]
                    p.act(bstg2[:], s2[:], AF.Copy, tag=f"{tg}Db2{jt}_{ch}")
                    p.dma("sync", dtu_scr[jt * 128:(jt + 1) * 128, csl], bstg2[:],
                          reads=[bstg2], writes=[f"{nm(dtu_scr)}#t{jt}"], tag=f"{tg}Dc{jt}_{ch}")
            p.barrier(tag=f"{tg}D_end")
            p.free(*sps[::-1], *xcbt[::-1])
            acols = p.sbuf(f"{tg}_acols", (128, 256), F32)
            for q4 in range(4):
                p.dma("sync", acols[:, q4 * 64:(q4 + 1) * 64],
                      W[f"L3_{pref}_A_log"].rearrange("(k dd) s -> (dd s) k", dd=8)[:, q4 * 64:(q4 + 1) * 64],
                      writes=[acols], tag=f"{tg}Ea{q4}", allow_slow_non_contiguous=True)
            p.act(acols[:], acols[:], AF.Exp, tag=f"{tg}Ee")
            p.ts(acols[:], acols[:], -1.0, None, ALU.mult, tag=f"{tg}En")
            dcol = load_vec_cols(C, W[f"L3_{pref}_D"], DI, f"{tg}_D")
            bbc = p.sbuf(f"{tg}_bbc", (128, S), F32)
            cbc2 = p.sbuf(f"{tg}_cbc", (128, S), F32)
            for ch in range(S // TCH):
                csl = slice(ch * TCH, (ch + 1) * TCH)
                bk = C.bank()
                p.mm(bk[:, 0:TCH], [(repB[:], brow[:, csl])], tag=f"{tg}Eb{ch}")
                p.copy(bbc[:, csl], bk[:, 0:TCH], tag=f"{tg}Ebe{ch}")
                bk2 = C.bank()
                p.mm(bk2[:, 0:TCH], [(repB[:], crow[:, csl])], tag=f"{tg}Ec{ch}")
                p.copy(cbc2[:, csl], bk2[:, 0:TCH], tag=f"{tg}Ece{ch}")
            dtrep = p.sbuf(f"{tg}_dtrep", (128, S), BF16)
            dturep = p.sbuf(f"{tg}_dturep", (128, S), BF16)
            asb = p.sbuf(f"{tg}_asb", (128, S), F32)
            d1sb = p.sbuf(f"{tg}_d1sb", (128, S), F32)
            hsb = p.sbuf(f"{tg}_hsb", (128, S), F32)
            prbf = p.sbuf(f"{tg}_prbf", (128, S), BF16)
            ybf = p.sbuf(f"{tg}_ybf", (128, TCH), BF16)
            xcr2 = p.sbuf(f"{tg}_xcr2", (128, TCH), F32)
            zr = p.sbuf(f"{tg}_zr", (128, TCH), F32)
            zs = p.sbuf(f"{tg}_zs", (128, TCH), F32)
            for dblk in range(NKT):
                ybanks = C.bank(4, hold=True)
                for lj in range(16):
                    k = dblk * 16 + lj
                    p.dma("sync", dtrep[:],
                          dt_scr.rearrange("(kk q) t -> kk q t", q=128)
                          [dblk, lj * 8:(lj + 1) * 8, :].unsqueeze(1).broadcast_to([8, 16, S]),
                          reads=[f"{nm(dt_scr)}#t{dblk}"], writes=[dtrep], tag=f"{tg}Gd{k}")
                    p.dma("sync", dturep[:],
                          dtu_scr.rearrange("(kk q) t -> kk q t", q=128)
                          [dblk, lj * 8:(lj + 1) * 8, :].unsqueeze(1).broadcast_to([8, 16, S]),
                          reads=[f"{nm(dtu_scr)}#t{dblk}"], writes=[dturep], tag=f"{tg}Gu{k}")
                    p.act(asb[:], dtrep[:], AF.Exp, scale=acols[:, k:k + 1], tag=f"{tg}Ga{k}")
                    p.tt(d1sb[:], dturep[:], bbc[:], ALU.mult, tag=f"{tg}G1{k}")
                    p.scan(hsb[:], asb[:], d1sb[:], 0.0, ALU.mult, ALU.add, tag=f"{tg}Gs{k}")
                    p.tt(prbf[:], hsb[:], cbc2[:], ALU.mult, eng="gpsimd", tag=f"{tg}Gp{k}")
                    for ch in range(S // TCH):
                        p.mm(ybanks[ch][:, 0:TCH],
                             [(selJ[:, lj * 128:(lj + 1) * 128], prbf[:, ch * TCH:(ch + 1) * TCH])],
                             start=(lj == 0), stop=(lj == 15), tag=f"{tg}Gy{k}_{ch}")
                for ch in range(S // TCH):
                    csl = slice(ch * TCH, (ch + 1) * TCH)
                    zsl = csl if not rev else slice(S - (ch + 1) * TCH, S - ch * TCH)
                    p.dma("sync", xcr2[:], xc_scr[dblk * 128:(dblk + 1) * 128, csl],
                          reads=[f"{nm(xc_scr)}#t{dblk}"], writes=[xcr2], tag=f"{tg}Gx{dblk}_{ch}")
                    p.dma("sync", zr[:], z_scr[dblk * 128:(dblk + 1) * 128, zsl],
                          reads=[f"{nm(z_scr)}#t{dblk}"], writes=[zr], tag=f"{tg}Gz{dblk}_{ch}")
                    p.stt(zs[:], xcr2[:], dcol[:, dblk:dblk + 1], ybanks[ch][:, 0:TCH],
                          ALU.mult, ALU.add, tag=f"{tg}Gv{dblk}_{ch}")
                    p.act(zr[:], zr[:], AF.Silu, tag=f"{tg}Gw{dblk}_{ch}")
                    p.tt(zs[:], zs[:], zr[:] if not rev else zr[:, ::-1], ALU.mult,
                         tag=f"{tg}Gm{dblk}_{ch}")
                    p.act(ybf[:], zs[:], AF.Copy, tag=f"{tg}Gc{dblk}_{ch}")
                    p.dma("sync", y_scr[dblk * 128:(dblk + 1) * 128, csl], ybf[:],
                          reads=[ybf], writes=[f"{nm(y_scr)}#t{dblk}"], tag=f"{tg}Gs2{dblk}_{ch}")
                C.release(*ybanks)
            p.barrier(tag=f"{tg}G_end")
            p.free(zs, zr, xcr2, ybf, prbf, hsb, d1sb, asb, dturep, dtrep,
                   cbc2, bbc, dcol, acols)
            ytl = [p.sbuf(f"{tg}_ytl{i}", (128, TCH), BF16) for i in range(NKT)]
            wstr2 = [p.sbuf(f"{tg}_wstr2{i}", (128, NKT * 128), BF16) for i in range(2)]
            for ch in range(S // TCH):
                csl = slice(ch * TCH, (ch + 1) * TCH)
                for kt in range(NKT):
                    p.dma("sync", ytl[kt][:], y_scr[kt * 128:(kt + 1) * 128, csl],
                          reads=[f"{nm(y_scr)}#t{kt}"], writes=[ytl[kt]], tag=f"{tg}Hl{ch}_{kt}")
                for j in range(NT):
                    wb = wstr2[j % 2]
                    for _k0 in range(0, NKT, 8):
                        p.op("gpsimd", (lambda e, k0=_k0, wb=wb, j=j, pref=pref: e.dma_start(
                            out=wb[:].rearrange("q (k c) -> q k c", k=NKT)[:, k0:k0 + 8, :],
                            in_=W[f"L3_{pref}_out_w_T"][:, j * 128:(j + 1) * 128]
                            .rearrange("(k q) c -> q k c", q=128)[:, k0:k0 + 8, :])),
                            reads=[], writes=[wb], dma=True, tag=f"{tg}Hw{ch}_{j}_{_k0}")
                    bk = C.bank()
                    p.mm(bk[:, 0:TCH],
                         [(wb[:, kt * 128:(kt + 1) * 128], ytl[kt][:]) for kt in range(NKT)],
                         tag=f"{tg}Hm{ch}_{j}")
                    s = stg[j % 2]
                    p.copy(s[:, 0:TCH], bk[:, 0:TCH], tag=f"{tg}He{ch}_{j}")
                    p.dma("sync", outd_scr[di, j * 128:(j + 1) * 128, csl], s[:, 0:TCH],
                          reads=[s], writes=[f"outd{di}#t{j}"], tag=f"{tg}Hs{ch}_{j}")
            p.barrier(tag=f"{tg}H_end")
            p.free(*wstr2[::-1], *ytl[::-1], crow, brow, xdbl_dt, dtb_c, dtwT, xwT)

        msum = p.sbuf(f"l3a_ms{b}", (1, S), F32)
        msq = p.sbuf(f"l3a_mq{b}", (1, S), F32)
        ofb = p.sbuf(f"l3a_ofb{b}", (128, TCH), F32)
        obb = p.sbuf(f"l3a_obb{b}", (128, TCH), F32)
        p.memset(msum[:], 0.0, tag=f"l3aI0{b}")
        p.memset(msq[:], 0.0, tag=f"l3aI1{b}")
        for ch in range(S // TCH):
            gch = b * (S // TCH) + ch
            sl = slice(gch * TCH, (gch + 1) * TCH)
            csl = slice(ch * TCH, (ch + 1) * TCH)
            rsl = slice(S - (ch + 1) * TCH, S - ch * TCH)
            for j in range(NT):
                rb = rb_pool[j % 2]
                p.dma("sync", rb[:], r_in[j * 128:(j + 1) * 128, sl],
                      reads=[rtile(r_in, j)], writes=[rb], tag=f"l3aIl{gch}_{j}")
                s = C.scr[j % 2]
                p.tt(s[:, 0:TCH], rb[:], C.abc[:, sl], ALU.mult, tag=f"l3aIm{gch}_{j}")
                p.tt(s[:, 0:TCH], s[:, 0:TCH], C.bbc[:, sl], ALU.add, tag=f"l3aIa{gch}_{j}")
                p.dma("sync", ofb[:], outd_scr[0, j * 128:(j + 1) * 128, csl],
                      reads=[f"outd0#t{j}"], writes=[ofb], tag=f"l3aIf{gch}_{j}")
                p.dma("sync", obb[:], outd_scr[1, j * 128:(j + 1) * 128, rsl],
                      reads=[f"outd1#t{j}"], writes=[obb], tag=f"l3aIb{gch}_{j}")
                p.stt(s[:, 0:TCH], ofb[:], 0.5, s[:, 0:TCH], ALU.mult, ALU.add, tag=f"l3aIo{gch}_{j}")
                p.stt(s[:, 0:TCH], obb[:, ::-1], 0.5, s[:, 0:TCH], ALU.mult, ALU.add, tag=f"l3aIr{gch}_{j}")
                p.dma("sync", xmid_scr[j * 128:(j + 1) * 128, sl], s[:, 0:TCH],
                      reads=[s], writes=[f"xmid#t{j}"], tag=f"l3aIs{gch}_{j}")
                sq = C.scr[2]
                p.act(sq[:, 0:TCH], s[:, 0:TCH], AF.Square, tag=f"l3aIq{gch}_{j}")
                b1 = C.bank()
                p.mm(b1[0:1, 0:TCH], [(C.onec[:], s[:, 0:TCH])], tag=f"l3aIs1{gch}_{j}")
                p.stt(msum[0:1, csl], b1[0:1, 0:TCH], 1.0, msum[0:1, csl], ALU.mult, ALU.add,
                      tag=f"l3aIt1{gch}_{j}")
                b2 = C.bank()
                p.mm(b2[0:1, 0:TCH], [(C.onec[:], sq[:, 0:TCH])], tag=f"l3aIs2{gch}_{j}")
                p.stt(msq[0:1, csl], b2[0:1, 0:TCH], 1.0, msq[0:1, csl], ALU.mult, ALU.add,
                      tag=f"l3aIt2{gch}_{j}")
        for ch in range(S // TCH):
            csl = slice(ch * TCH, (ch + 1) * TCH)
            s0 = C.rscr[0:1, 0:TCH]
            s1 = C.rscr[0:1, TCH:2 * TCH]
            p.ts(s0, msum[0:1, csl], 1.0 / D, None, ALU.mult, tag=f"l3aJm{b}_{ch}")
            p.ts(s1, msq[0:1, csl], 1.0 / D, None, ALU.mult, tag=f"l3aJq{b}_{ch}")
            p.act(msq[0:1, csl], s0, AF.Square, tag=f"l3aJ2{b}_{ch}")
            p.tt(s1, s1, msq[0:1, csl], ALU.subtract, tag=f"l3aJv{b}_{ch}")
            p.ts(s1, s1, EPS, None, ALU.add, tag=f"l3aJe{b}_{ch}")
            p.act(s1, s1, AF.Sqrt, tag=f"l3aJs{b}_{ch}")
            p.recip(msum[0:1, csl], s1, tag=f"l3aJa{b}_{ch}")
            p.tt(msq[0:1, csl], s0, msum[0:1, csl], ALU.mult, tag=f"l3aJb{b}_{ch}")
            p.ts(msq[0:1, csl], msq[0:1, csl], -1.0, None, ALU.mult, tag=f"l3aJn{b}_{ch}")
        p.barrier(tag=f"l3aI_end{b}")
        p.free(obb, ofb)
        lng = load_vec_cols(C, W["L3_ff_ln_g"], D, f"l3a_lng{b}")
        lnb = load_vec_cols(C, W["L3_ff_ln_b"], D, f"l3a_lnb{b}")
        fb1s = [p.sbuf(f"l3a_fb1s{b}_{i}", (1, 128), BF16) for i in range(2)]
        fb2 = p.sbuf(f"l3a_fb2{b}", (1, D), BF16)
        p.dma("gpsimd", fb2[:], W["L3_ff_b2"].unsqueeze(0), writes=[fb2], tag=f"l3aJf2{b}")
        xnb = [p.sbuf(f"l3a_xnb{b}_{i}", (128, S), BF16) for i in range(NT)]
        for ch in range(S // TCH):
            csl = slice(ch * TCH, (ch + 1) * TCH)
            ba = C.bank()
            p.mm(ba[:, 0:TCH], [(C.onec1[:], msum[0:1, csl])], tag=f"l3aJba{b}_{ch}")
            bb2 = C.bank()
            p.mm(bb2[:, 0:TCH], [(C.onec1[:], msq[0:1, csl])], tag=f"l3aJbb{b}_{ch}")
            for j in range(NT):
                rb = rb_pool[j % 2]
                p.dma("sync", rb[:], xmid_scr[j * 128:(j + 1) * 128,
                                              b * S + ch * TCH:b * S + (ch + 1) * TCH],
                      reads=[f"xmid#t{j}"], writes=[rb], tag=f"l3aJl{b}_{ch}_{j}")
                s = C.scr[j % 2]
                p.tt(s[:, 0:TCH], rb[:], ba[:, 0:TCH], ALU.mult, tag=f"l3aJ1{b}_{ch}_{j}")
                p.tt(s[:, 0:TCH], s[:, 0:TCH], bb2[:, 0:TCH], ALU.add, tag=f"l3aJ2b{b}_{ch}_{j}")
                p.ts(xnb[j][:, csl], s[:, 0:TCH], lng[:, j:j + 1], lnb[:, j:j + 1],
                     ALU.mult, ALU.add, tag=f"l3aJ3{b}_{ch}_{j}")
        hsb2 = [p.sbuf(f"l3a_h{b}_{i}", (128, TCH), BF16) for i in range(32)]
        w2b = [p.sbuf(f"l3a_w2b{b}_{i}", (128, 32 * 128), BF16) for i in range(1)]
        for ch in range(S // TCH):
            csl = slice(ch * TCH, (ch + 1) * TCH)
            gch = b * (S // TCH) + ch
            sl = slice(gch * TCH, (gch + 1) * TCH)
            for jh in range(32):
                wb = wstr[jh % 2]
                p.dma("gpsimd", wb[:].rearrange("q (k c) -> q k c", k=NT),
                      W["L3_ff_w1_T"][:, jh * 128:(jh + 1) * 128]
                      .rearrange("(k q) c -> q k c", q=128),
                      writes=[wb], tag=f"l3aJw{b}_{ch}_{jh}")
                bk = C.bank()
                fbs = fb1s[jh % 2]
                p.dma("gpsimd", fbs[:], W["L3_ff_b1"][jh * 128:(jh + 1) * 128].unsqueeze(0),
                      writes=[fbs], tag=f"l3aJfb{b}_{ch}_{jh}")
                pairs = [(wb[:, kc * 128:(kc + 1) * 128], xnb[kc][:, csl]) for kc in range(NT)]
                pairs.append((fbs[0:1, :], onerb[:]))
                p.mm(bk[:, 0:TCH], pairs, tag=f"l3aJh{b}_{ch}_{jh}")
                p.act(hsb2[jh][:], bk[:, 0:TCH], AF.Silu, tag=f"l3aJe{b}_{ch}_{jh}")
            for j in range(NT):
                wb = w2b[0]
                p.dma("gpsimd", wb[:].rearrange("q (k c) -> q k c", k=32),
                      W["L3_ff_w2_T"][:, j * 128:(j + 1) * 128]
                      .rearrange("(k q) c -> q k c", q=128),
                      writes=[wb], tag=f"l3aJw2{b}_{ch}_{j}")
                bk = C.bank()
                pairs = [(wb[:, kc * 128:(kc + 1) * 128], hsb2[kc][:]) for kc in range(32)]
                pairs.append((fb2[0:1, j * 128:(j + 1) * 128], onerb[:]))
                p.mm(bk[:, 0:TCH], pairs, tag=f"l3aJo{b}_{ch}_{j}")
                rb = rb_pool[j % 2]
                p.dma("sync", rb[:], r_in[j * 128:(j + 1) * 128, sl],
                      reads=[rtile(r_in, j)], writes=[rb], tag=f"l3aTl{gch}_{j}")
                s = C.scr[j % 2]
                p.tt(s[:, 0:TCH], rb[:], C.abc[:, sl], ALU.mult, tag=f"l3aT1{gch}_{j}")
                p.tt(s[:, 0:TCH], s[:, 0:TCH], C.bbc[:, sl], ALU.add, tag=f"l3aT2{gch}_{j}")
                xm = robuf[0]
                p.dma("sync", xm[:], xmid_scr[j * 128:(j + 1) * 128, sl],
                      reads=[f"xmid#t{j}"], writes=[xm], tag=f"l3aTx{gch}_{j}")
                p.tt(s[:, 0:TCH], s[:, 0:TCH], xm[:], ALU.add, tag=f"l3aT3{gch}_{j}")
                ro = robuf[1]
                p.tt(ro[:], s[:, 0:TCH], bk[:, 0:TCH], ALU.add, tag=f"l3aT4{gch}_{j}")
                C.tail_chunk(j, gch, ro[:])
                p.dma("sync", r_out[j * 128:(j + 1) * 128, sl], ro[:], reads=[ro],
                      writes=[rtile(r_out, j)], store=True, tag=f"l3aTs{gch}_{j}")
        p.barrier(tag=f"l3aJ_end{b}")
        p.free(*w2b[::-1], *hsb2[::-1], *xnb[::-1], fb2, *fb1s[::-1], lnb, lng, msq, msum)
    C.tail_finish()


_register()



BLOCK_ORDER = ["L0_attn", "L0_ffn", "L1_attn", "L1_ffn",
               "L2_attn", "L2_ffn", "L3_attn", "L3_ffn"]


def build_finalize(consts_np):
    nc = bass.Bass("TRN2", target_bir_lowering=False, debug=False)
    r_in = nc.dram_tensor("r_in", [D, T], F32, kind="ExternalInput").ap()
    rows_in = nc.dram_tensor("rows_in", [2, T], F32, kind="ExternalInput").ap()
    x_out = nc.dram_tensor("x_out", [D, T], F32, kind="ExternalOutput").ap()
    consts_aps = declare_consts(nc, consts_np)
    p = Prog(nc)
    C = Ctx(p, consts_aps, need_xbf=False, need_rw=False)
    C.load_consts()
    p.dma("sync", C.rA[:], rows_in[0:1, :], writes=[C.rA], tag="rows_ld_a")
    p.dma("sync", C.rB[:], rows_in[1:2, :], writes=[C.rB], tag="rows_ld_b")
    emit_finalize(C, r_in, x_out)
    p.finish()
    p.lower()
    p.close_sbuf()
    p.es.close()
    return nc


def _wdtypes(wprep, names):
    out = {}
    for k in names:
        a = wprep[k]
        out[k] = (a.shape, BF16 if a.dtype == ml_dtypes.bfloat16 else F32)
    return out


def _reshard_to_bs(rs, rows):
    full = np.concatenate([r.reshape(D, BPC, S) for r in rs], axis=1)
    rfull = np.concatenate([w.reshape(2, BPC, S) for w in rows], axis=1)
    rs2, rows2 = [], []
    for c in range(NCORES):
        sl = slice(c * (S // NCORES), (c + 1) * (S // NCORES))
        rs2.append(np.ascontiguousarray(full[:, :, sl].reshape(D, T)))
        rows2.append(np.ascontiguousarray(rfull[:, :, sl].reshape(2, T)))
    return rs2, rows2


def _reshard_to_batch(rs, rows):
    vw = S // NCORES
    full = np.empty((D, B_FULL, S), np.float32)
    rfull = np.empty((2, B_FULL, S), np.float32)
    for c in range(NCORES):
        full[:, :, c * vw:(c + 1) * vw] = rs[c].reshape(D, B_FULL, vw)
        rfull[:, :, c * vw:(c + 1) * vw] = rows[c].reshape(2, B_FULL, vw)
    rs2 = [np.ascontiguousarray(full[:, c * BPC:(c + 1) * BPC].reshape(D, T))
           for c in range(NCORES)]
    rows2 = [np.ascontiguousarray(rfull[:, c * BPC:(c + 1) * BPC].reshape(2, T))
             for c in range(NCORES)]
    return rs2, rows2


_BUILD_CACHE = {}




def emit_L2_attn_fused(C, W, r_in, r_out):
    p = C.p
    nc = p.nc
    H, DH = 16, 64
    EPS = 1e-5
    q_scr = nc.dram_tensor("l2b_qscr", [D, T], BF16, kind="Internal").ap()
    k_loc = nc.dram_tensor("l2b_kloc", [D, T], BF16, kind="Internal").ap()
    v_loc = nc.dram_tensor("l2b_vloc", [D, T], BF16, kind="Internal").ap()
    k_all = nc.dram_tensor("l2b_kall", [NCORES * D, T], BF16, kind="Internal").ap()
    v_all = nc.dram_tensor("l2b_vall", [NCORES * D, T], BF16, kind="Internal").ap()
    ao_scr = nc.dram_tensor("l2b_aoscr", [D, T], BF16, kind="Internal").ap()

    C.rows_to_bcast()
    C.stats_reset()

    rb_pool = [p.sbuf(f"l2b_rb{i}", (128, TCH), F32) for i in range(2)]
    robuf = [p.sbuf(f"l2b_ro{i}", (128, TCH), F32) for i in range(2)]
    lng = load_vec_cols(C, W["L2_attn_ln_g"], D, "l2b_lng")
    lnb = load_vec_cols(C, W["L2_attn_ln_b"], D, "l2b_lnb")
    biases = {}
    for bn in ("bq", "bk", "bv", "bo"):
        t = p.sbuf(f"l2b_{bn}", (1, D), BF16)
        p.dma("gpsimd", t[:], W[f"L2_attn_{bn}"].unsqueeze(0), writes=[t], tag=f"l2b_bld_{bn}")
        biases[bn] = t
    onerb = p.sbuf("l2b_onerb", (1, TCH), BF16)
    p.dma("sync", onerb[:], C.consts["ones_row_bf"], writes=[onerb], tag="l2b_onerb")
    hmaskb = p.sbuf("l2b_hmb", (128, 128), BF16)
    p.dma("gpsimd", hmaskb[:], C.consts["hmask16"], writes=[hmaskb], tag="l2b_hmb")
    hmTb = p.sbuf("l2b_hmTb", (16, 128 * NT), BF16)
    p.dma("gpsimd", hmTb[:], C.consts["hmaskT16"], writes=[hmTb], tag="l2b_hmTb")

    cbc = p.sbuf("l2b_cbc", (128, T), F32)
    s0 = C.rscr[0:1, 0:TCH]
    for c in range(NCH):
        sl = slice(c * TCH, (c + 1) * TCH)
        p.act(s0, C.rA[0:1, sl], AF.Square, tag=f"l2b_c2{c}")
        p.ts(s0, s0, -EPS, 1.0 + EPS, ALU.mult, ALU.add, tag=f"l2b_c3{c}")
        p.act(s0, s0, AF.Sqrt, tag=f"l2b_c4{c}")
        p.recip(s0, s0, tag=f"l2b_c5{c}")
        bk = C.bank()
        p.mm(bk[:, 0:TCH], [(C.onec1[:], s0)], tag=f"l2b_cb{c}")
        p.copy(cbc[:, sl], bk[:, 0:TCH], tag=f"l2b_cbe{c}")

    p.push_scope()
    wq = load_w_ft(C, W["L2_attn_Wq"], D, D, "l2b_wq")
    wk = load_w_ft(C, W["L2_attn_Wk"], D, D, "l2b_wk")
    wv = load_w_ft(C, W["L2_attn_Wv"], D, D, "l2b_wv")
    xbf_c = [p.sbuf(f"l2b_xc{i}", (128, TCH), BF16) for i in range(NT)]
    stage = [p.sbuf(f"l2b_stg{i}", (128, TCH), BF16) for i in range(3)]
    for gch in range(NCH):
        sl = slice(gch * TCH, (gch + 1) * TCH)
        for kc in range(NT):
            rb = rb_pool[kc % 2]
            p.dma("sync", rb[:], r_in[kc * 128:(kc + 1) * 128, sl],
                  reads=[rtile(r_in, kc)], writes=[rb], tag=f"l2b_ld{gch}_{kc}")
            s = C.scr[kc % 2]
            p.tt(s[:, 0:TCH], rb[:], C.abc[:, sl], ALU.mult, tag=f"l2b_m{gch}_{kc}")
            p.tt(s[:, 0:TCH], s[:, 0:TCH], C.bbc[:, sl], ALU.add, tag=f"l2b_a{gch}_{kc}")
            p.tt(s[:, 0:TCH], s[:, 0:TCH], cbc[:, sl], ALU.mult, tag=f"l2b_cn{gch}_{kc}")
            p.ts(xbf_c[kc][:], s[:, 0:TCH], lng[:, kc:kc + 1], lnb[:, kc:kc + 1],
                 ALU.mult, ALU.add, tag=f"l2b_wn{gch}_{kc}")
        for j in range(NT):
            for wt, bkey, scr_, st_i, scale in ((wq, "bq", q_scr, 0, 0.125), (wk, "bk", k_loc, 1, 1.0), (wv, "bv", v_loc, 2, 1.0)):
                bk = C.bank()
                pairs = [(wt[kc][:, j * 128:(j + 1) * 128], xbf_c[kc][:]) for kc in range(NT)]
                pairs.append((biases[bkey][0:1, j * 128:(j + 1) * 128], onerb[:]))
                p.mm(bk[:, 0:TCH], pairs, tag=f"l2b_p{gch}_{j}_{st_i}")
                stg = stage[st_i]
                p.act(stg[:], bk[:, 0:TCH], AF.Copy, scale=scale, tag=f"l2b_pe{gch}_{j}_{st_i}")
                p.dma("sync", scr_[j * 128:(j + 1) * 128, sl], stg[:],
                      reads=[stg], writes=[f"{nm(scr_)}#t{j}"], tag=f"l2b_ps{gch}_{j}_{st_i}")
    p.barrier(tag="l2b_ph1_end")
    p.pop_scope()

    def emit_ag(e, src_=k_loc, dst_=k_all):
        return e.collective_compute(
            "AllGather", ALU.bypass, replica_groups=[list(range(NCORES))],
            ins=[src_.opt()], outs=[dst_.opt()])
    p.op("gpsimd", lambda e: emit_ag(e, k_loc, k_all),
         reads=[f"{nm(k_loc)}#t{j}" for j in range(NT)], writes=[nm(k_all)], tag="l2b_agk")
    p.op("gpsimd", lambda e: emit_ag(e, v_loc, v_all),
         reads=[f"{nm(v_loc)}#t{j}" for j in range(NT)], writes=[nm(v_all)], tag="l2b_agv")

    p.push_scope()
    NSB = 16
    qch = [p.sbuf(f"l2b_qc{i}", (128, TCH), BF16) for i in range(NT)]
    kch = [p.sbuf(f"l2b_kc{i}", (128, TCH), BF16) for i in range(2 * NT)]
    vch = [p.sbuf(f"l2b_vc{i}", (128, TCH), BF16) for i in range(2)]
    prod = p.sbuf("l2b_prod", (128, TCH), BF16)
    scs = p.sbuf("l2b_scs", (16, NSB * TCH), F32)
    wbf = p.sbuf("l2b_wbf", (16, NSB * TCH), BF16)
    mxr = p.sbuf("l2b_mxr", (16, TCH), F32)
    avs = p.sbuf("l2b_avs", (128, TCH), F32)
    tsb = p.sbuf("l2b_tsb", (128, TCH), F32)
    aos = p.sbuf("l2b_aos", (128, TCH), BF16)

    for sl_loc in range(BPC):
        for vc_ in range(S // TCH):
            sl = slice(sl_loc * S + vc_ * TCH, sl_loc * S + vc_ * TCH + TCH)
            for j in range(NT):
                p.dma("sync", qch[j][:], q_scr[j * 128:(j + 1) * 128, sl],
                      reads=[f"{nm(q_scr)}#t{j}"], writes=[qch[j]], tag=f"l2b_ql{sl_loc}_{vc_}_{j}")
            for s2 in range(NSB):
                roff = (s2 // 2) * D
                csl = slice((s2 % 2) * S + vc_ * TCH, (s2 % 2) * S + vc_ * TCH + TCH)
                bkp = C.bank(hold=True)
                for j in range(NT):
                    kc_ = kch[(s2 % 2) * NT + j]
                    p.dma("sync", kc_[:], k_all[roff + j * 128:roff + (j + 1) * 128, csl],
                          reads=[nm(k_all)], writes=[kc_], tag=f"l2b_kl{sl_loc}_{vc_}_{s2}_{j}")
                    p.tt(prod[:], qch[j][:], kc_[:], ALU.mult, tag=f"l2b_pr{sl_loc}_{vc_}_{s2}_{j}")
                    p.mm(bkp[0:16, 0:TCH], [(hmaskb[:, j * 16:(j + 1) * 16], prod[:])],
                         start=(j == 0), stop=(j == NT - 1), tag=f"l2b_sc{sl_loc}_{vc_}_{s2}_{j}")
                p.copy(scs[:, s2 * TCH:(s2 + 1) * TCH], bkp[0:16, 0:TCH], tag=f"l2b_sce{sl_loc}_{vc_}_{s2}")
                C.release(bkp)
            v3 = scs[:].rearrange("h (p v) -> h v p", p=NSB)
            p.reduce(mxr[0:16, :], v3, ALU.max, tag=f"l2b_mx{sl_loc}_{vc_}")
            for pp in range(NSB):
                pl = scs[:, pp * TCH:(pp + 1) * TCH]
                p.tt(pl, pl, mxr[0:16, :], ALU.subtract, tag=f"l2b_sb{sl_loc}_{vc_}_{pp}")
                p.act(pl, pl, AF.Exp, tag=f"l2b_ex{sl_loc}_{vc_}_{pp}")
            p.reduce(mxr[0:16, :], v3, ALU.add, tag=f"l2b_sm{sl_loc}_{vc_}")
            p.recip(mxr[0:16, :], mxr[0:16, :], tag=f"l2b_rs{sl_loc}_{vc_}")
            for pp in range(NSB):
                pl = scs[:, pp * TCH:(pp + 1) * TCH]
                p.tt(wbf[:, pp * TCH:(pp + 1) * TCH], pl, mxr[0:16, :], ALU.mult,
                     tag=f"l2b_nw{sl_loc}_{vc_}_{pp}")
            for j in range(NT):
                p.memset(avs[:], 0.0, tag=f"l2b_av0{sl_loc}_{vc_}_{j}")
                for s2 in range(NSB):
                    roff = (s2 // 2) * D
                    csl = slice((s2 % 2) * S + vc_ * TCH, (s2 % 2) * S + vc_ * TCH + TCH)
                    bkw = C.bank()
                    p.mm(bkw[:, 0:TCH], [(hmTb[:, j * 128:(j + 1) * 128],
                                          wbf[:, s2 * TCH:(s2 + 1) * TCH])],
                         tag=f"l2b_wb{sl_loc}_{vc_}_{j}_{s2}")
                    vcb = vch[s2 % 2]
                    p.dma("sync", vcb[:], v_all[roff + j * 128:roff + (j + 1) * 128, csl],
                          reads=[nm(v_all)], writes=[vcb], tag=f"l2b_vl{sl_loc}_{vc_}_{j}_{s2}")
                    p.tt(tsb[:], vcb[:], bkw[:, 0:TCH], ALU.mult, tag=f"l2b_vm{sl_loc}_{vc_}_{j}_{s2}")
                    p.tt(avs[:], avs[:], tsb[:], ALU.add, tag=f"l2b_va{sl_loc}_{vc_}_{j}_{s2}")
                p.copy(aos[:], avs[:], tag=f"l2b_aoc{sl_loc}_{vc_}_{j}")
                p.dma("sync", ao_scr[j * 128:(j + 1) * 128, sl], aos[:],
                      reads=[aos], writes=[f"aoscr2#{j}"], tag=f"l2b_aos{sl_loc}_{vc_}_{j}")
    p.barrier(tag="l2b_ph2_end")
    p.pop_scope()

    p.push_scope()
    wo = load_w_ft(C, W["L2_attn_Wo"], D, D, "l2b_wo")
    aoc = [p.sbuf(f"l2b_aoc2{i}", (128, TCH), BF16) for i in range(NT)]
    for gch in range(NCH):
        sl = slice(gch * TCH, (gch + 1) * TCH)
        for kc in range(NT):
            p.dma("sync", aoc[kc][:], ao_scr[kc * 128:(kc + 1) * 128, sl],
                  reads=[f"aoscr2#{kc}"], writes=[aoc[kc]], tag=f"l2b_aol{gch}_{kc}")
        for j in range(NT):
            bk = C.bank()
            pairs = [(wo[kc][:, j * 128:(j + 1) * 128], aoc[kc][:]) for kc in range(NT)]
            pairs.append((biases["bo"][0:1, j * 128:(j + 1) * 128], onerb[:]))
            p.mm(bk[:, 0:TCH], pairs, tag=f"l2b_o{gch}_{j}")
            rb = rb_pool[j % 2]
            p.dma("sync", rb[:], r_in[j * 128:(j + 1) * 128, sl],
                  reads=[rtile(r_in, j)], writes=[rb], tag=f"l2b_tl{gch}_{j}")
            s = C.scr[j % 2]
            p.tt(s[:, 0:TCH], rb[:], C.abc[:, sl], ALU.mult, tag=f"l2b_t1{gch}_{j}")
            p.tt(s[:, 0:TCH], s[:, 0:TCH], C.bbc[:, sl], ALU.add, tag=f"l2b_t2{gch}_{j}")
            ro = robuf[j % 2]
            p.tt(ro[:], s[:, 0:TCH], s[:, 0:TCH], ALU.add, tag=f"l2b_t3{gch}_{j}")
            p.tt(ro[:], ro[:], bk[:, 0:TCH], ALU.add, tag=f"l2b_t4{gch}_{j}")
            C.tail_chunk(j, gch, ro[:])
            p.dma("sync", r_out[j * 128:(j + 1) * 128, sl], ro[:], reads=[ro],
                  writes=[rtile(r_out, j)], store=True, tag=f"l2b_st{gch}_{j}")
    p.pop_scope()
    C.tail_finish()



def build_fused(wprep, consts_np):
    nc = bass.Bass("TRN2", target_bir_lowering=False, debug=False)
    x_in = nc.dram_tensor("x_in", [D, T], F32, kind="ExternalInput").ap()
    x_out = nc.dram_tensor("x_out", [D, T], F32, kind="ExternalOutput").ap()
    consts_aps = declare_consts(nc, consts_np)
    W = {}
    for blk in BLOCK_ORDER:
        for k in BLOCK_WEIGHTS[blk]:
            if k in W:
                continue
            a = wprep[k]
            dt = BF16 if a.dtype == ml_dtypes.bfloat16 else F32
            W[k] = nc.dram_tensor(f"w_{k}", list(a.shape), dt, kind="ExternalInput").ap()
    rp = [nc.dram_tensor(f"rping{i}", [D, T], F32, kind="Internal").ap() for i in range(2)]

    p = Prog(nc)
    C = Ctx(p, consts_aps, need_xbf=False, need_rw=False)
    C.load_consts()
    p.memset(C.rA[:], 1.0, tag="rows0a")
    p.memset(C.rB[:], 0.0, tag="rows0b")
    cur = x_in
    for i, blk in enumerate(BLOCK_ORDER):
        nxt = rp[i % 2]
        p.push_scope()
        emitter = emit_L2_attn_fused if blk == "L2_attn" else BLOCK_EMITTERS[blk]
        emitter(C, W, cur, nxt)
        p.barrier(tag=f"blk_end_{blk}")
        p.pop_scope()
        cur = nxt
    p.push_scope()
    emit_finalize(C, cur, x_out)
    p.pop_scope()
    p.finish()
    p.lower()
    p.close_sbuf()
    p.es.close()
    return nc


FUSED = True


def kernel(**inputs):
    wprep = prep_weights(inputs)
    consts = make_consts()
    x = np.asarray(inputs["L0_attn_x"], np.float32)
    rs = host_shard_x(x)
    cm = {f"c_{k}": v for k, v in consts.items()}
    if FUSED:
        if "fused" not in _BUILD_CACHE:
            _BUILD_CACHE["fused"] = build_fused(wprep, consts)
        nc = _BUILD_CACHE["fused"]
        wm = {}
        for blk in BLOCK_ORDER:
            for k in BLOCK_WEIGHTS[blk]:
                wm[f"w_{k}"] = wprep[k]
        in_maps = [{"x_in": rs[c], **cm, **wm} for c in range(NCORES)]
        res = run_bass_kernel_spmd(nc, in_maps, core_ids=list(range(NCORES)))
        return host_unshard_x([np.asarray(res.results[c]["x_out"]) for c in range(NCORES)])
    return _kernel_unfused(inputs, wprep, consts, rs, cm)


def _kernel_unfused(inputs, wprep, consts, rs, cm):
    rows = [np.ascontiguousarray(np.stack([np.ones(T, np.float32), np.zeros(T, np.float32)]))
            for _ in range(NCORES)]
    for blk in BLOCK_ORDER:
        if blk == "L2_attn":
            rs, rows = _reshard_to_bs(rs, rows)
        if blk not in _BUILD_CACHE:
            _BUILD_CACHE[blk] = build_single_block(
                blk, _wdtypes(wprep, BLOCK_WEIGHTS[blk]), consts)
        nc = _BUILD_CACHE[blk]
        wm = {f"w_{k}": wprep[k] for k in BLOCK_WEIGHTS[blk]}
        in_maps = [{"r_in": rs[c], "rows_in": rows[c], **cm, **wm} for c in range(NCORES)]
        res = run_bass_kernel_spmd(nc, in_maps, core_ids=list(range(NCORES)))
        rs = [np.asarray(res.results[c]["r_out"]) for c in range(NCORES)]
        rows = [np.asarray(res.results[c]["rows_out"]) for c in range(NCORES)]
        if blk == "L2_attn":
            rs, rows = _reshard_to_batch(rs, rows)
    if "fin" not in _BUILD_CACHE:
        _BUILD_CACHE["fin"] = build_finalize(consts)
    nc = _BUILD_CACHE["fin"]
    in_maps = [{"r_in": rs[c], "rows_in": rows[c], **cm} for c in range(NCORES)]
    res = run_bass_kernel_spmd(nc, in_maps, core_ids=list(range(NCORES)))
    return host_unshard_x([np.asarray(res.results[c]["x_out"]) for c in range(NCORES)])
